# version 02 (final)
# baseline (speedup 1.0000x reference)
import math
import os
import contextlib
DEBUG = bool(os.environ.get("KMK_DEBUG"))
import numpy as np
import ml_dtypes
import concourse.bass as bass
import concourse.mybir as mybir
from concourse.bass_utils import run_bass_kernel_spmd

F32 = mybir.dt.float32
BF16 = mybir.dt.bfloat16
AF = mybir.ActivationFunctionType
ALU = mybir.AluOpType
AX = mybir.AxisListType.X

B, SEQ, H = 2, 4096, 2048
TLOC, NT, NH = 1024, 8, 16
HEADS, KVH, DH = 16, 8, 128
NE, NEALL, IEXP = 32, 34, 1024
TOPK = 1638
INV_SQRT_D = 1.0 / math.sqrt(DH)
BF = ml_dtypes.bfloat16
CCR = 2081


class Pr:
    ENG = ("t", "v", "s", "g", "y")

    def __init__(self):
        self.q = {e: [] for e in self.ENG}
        self.cnt = {}
        self.lastw = {}

    def inc(self, sem, amt):
        self.cnt[sem] = self.cnt.get(sem, 0) + amt
        return (sem, self.cnt[sem])

    def wait(self, e, tok):
        if tok is None:
            return
        sem, val = tok
        assert val <= self.cnt.get(sem, 0), f"wait on future {sem} {val}"
        key = (e, sem)
        if self.lastw.get(key, -1) >= val:
            return
        self.lastw[key] = val
        self.q[e].append(("WAIT", sem, val))


def build_nc():
    nc = bass.Bass("TRN2", target_bir_lowering=False)
    P = Pr()

    def din(name, shape, dt=F32):
        return nc.dram_tensor(name, list(shape), dt, kind="ExternalInput").ap()

    x_d = din("x", (TLOC, H))
    w1_d = din("w1t", (NEALL, H, IEXP))
    w3_d = din("w3t", (NEALL, H, IEXP))
    w2_d = din("w2t", (NEALL, IEXP, H))
    gate_d = din("gatet", (H, NE))
    wq_d = din("wq", (H, H))
    wk_d = din("wk", (H, KVH * DH))
    wv_d = din("wv", (H, KVH * DH))
    wo_d = din("wo", (H, H))
    wr_d = din("wr", (H, HEADS))
    bq_d = din("bq", (1, H))
    bk_d = din("bk", (1, KVH * DH))
    bv_d = din("bv", (1, KVH * DH))
    bo_d = din("bo", (1, H))
    br_d = din("br", (1, HEADS))
    cosh_d = din("cosh", (TLOC, 64))
    sinn_d = din("sinn", (TLOC, 64))
    sinp_d = din("sinp", (TLOC, 64))
    iden_d = din("iden", (128, 128), BF16)
    idf_d = din("idf32", (128, 128), F32)
    onesq_d = din("onesq", (128, 128), BF16)
    onesrow_d = din("onesrow", (1, SEQ), BF16)
    blk8_d = din("blk8", (128, 128), F32)
    out_d = nc.dram_tensor("out", [TLOC, H], F32, kind="ExternalOutput").ap()
    if DEBUG:
        d_x1 = nc.dram_tensor("d_x1", [TLOC, H], F32, kind="ExternalOutput").ap()
        d_rl = nc.dram_tensor("d_rl", [128, 512], F32, kind="ExternalOutput").ap()
        d_thr = nc.dram_tensor("d_thr", [128, 8], F32, kind="ExternalOutput").ap()
        d_kx = nc.dram_tensor("d_kx", [1, 32], F32, kind="ExternalOutput").ap()
        d_att = nc.dram_tensor("d_att", [128, NH * TLOC], F32, kind="ExternalOutput").ap()
        d_qt = nc.dram_tensor("d_qt", [128, NH * TLOC], F32, kind="ExternalOutput").ap()
        d_kt = nc.dram_tensor("d_kt", [128, 4096], F32, kind="ExternalOutput").ap()
        d_x2 = nc.dram_tensor("d_x2", [TLOC, H], F32, kind="ExternalOutput").ap()
        d_xt = nc.dram_tensor("d_xt", [128, NH * TLOC], F32, kind="ExternalOutput").ap()
        d_comb = nc.dram_tensor("d_comb", [128, NT * 32], F32, kind="ExternalOutput").ap()
        d_ht = nc.dram_tensor("d_ht", [128, 8192], F32, kind="ExternalOutput").ap()

    ccin = nc.dram_tensor("ccin", [CCR, TLOC], BF16, kind="Internal")
    ccout = nc.dram_tensor("ccout", [4 * CCR, TLOC], BF16, kind="Internal")
    masknd = nc.dram_tensor("masknd", [128, 512], BF16, kind="Internal")
    krowd = nc.dram_tensor("krowd", [8, 1], F32, kind="Internal")

    ctx = contextlib.ExitStack()
    sbuf = lambda n, shp, dt: ctx.enter_context(nc.sbuf_tensor(n, list(shp), dt))
    psum = lambda n, shp, dt: ctx.enter_context(nc.psum_tensor(n, list(shp), dt))

    with ctx:
        X = sbuf("X", (128, NT * H), F32)
        XT = sbuf("XT", (128, NH * TLOC), BF16)
        U = sbuf("U", (128, 36864), BF16)
        XLN = sbuf("XLN", (128, H), BF16)
        COSH = sbuf("COSH", (128, NT * 64), F32)
        SINN = sbuf("SINN", (128, NT * 64), F32)
        SINP = sbuf("SINP", (128, NT * 64), F32)
        SILF = sbuf("SILF", (128, 512), F32)
        ROPA = sbuf("ROPA", (128, 256), F32)
        ROPB = sbuf("ROPB", (128, 256), F32)
        LOGIT = sbuf("LOGIT", (128, NT * 32), F32)
        GSC = sbuf("GSC", (128, 32), F32)
        GT1 = sbuf("GT1", (128, 32), F32)
        GT2 = sbuf("GT2", (128, 32), F32)
        GT3 = sbuf("GT3", (128, 32), F32)
        GG = sbuf("GG", (128, 16), F32)
        GH = sbuf("GH", (128, 8), F32)
        GH2 = sbuf("GH2", (128, 8), F32)
        GS = sbuf("GS", (128, 8), F32)
        COMB = sbuf("COMB", (128, NT * 32), F32)
        ST1 = sbuf("ST1", (128, 8), F32)
        RL = sbuf("RL", (128, 512), F32)
        RTHR = sbuf("RTHR", (128, 8), F32)
        RECIP = sbuf("RECIP", (128, 512), F32)
        RLOG = sbuf("RLOG", (128, NT * 16), F32)
        RSM = sbuf("RSM", (16, 128), F32)
        KN2L = sbuf("KN2L", (128, 8), F32)
        KN2S = sbuf("KN2S", (128, 8), F32)
        KROW = sbuf("KROW", (1, 8), F32)
        KN2G = sbuf("KN2G", (1, 32), F32)
        KMAX2 = sbuf("KMAX2", (1, 8), F32)
        CC12 = sbuf("CC12", (1, 32), F32)
        BLK8 = sbuf("BLK8", (128, 128), F32)
        BIAS2 = sbuf("BIAS2", (2, SEQ), BF16)
        RHS2 = sbuf("RHS2", (2, TLOC), BF16)
        NMB = sbuf("NMB", (1, TLOC), BF16)
        ONES = sbuf("ONES", (128, 128), BF16)
        IDEN = sbuf("IDEN", (128, 128), BF16)
        IDF = sbuf("IDF", (128, 128), F32)
        GATE = sbuf("GATE", (128, NH * 32), BF16)
        WRS = sbuf("WRS", (128, NH * 16), BF16)
        BR1 = sbuf("BR1", (1, 512), BF16)
        BRR = sbuf("BRR", (1, 16), BF16)

        def uv(off, n):
            return U[:, off:off + n]
        W13 = [uv(4096 * i, 4096) for i in range(3)]
        W2B = [uv(12288 + 4096 * i, 4096) for i in range(2)]
        HT = [uv(20480 + 8192 * i, 8192) for i in range(2)]
        QT = uv(0, 16384)
        KTKV = uv(16384, 4096)
        VKV = uv(20480, 4096)
        WS = [uv(24576 + 4096 * i, 4096) for i in range(2)]
        KSTG = uv(32768, 2048)
        VSTG = uv(34816, 256)
        QSQ = uv(32768, 1024)
        KSQ2 = uv(35072, 256)
        PTE = uv(33792, 1024)
        MASKB = uv(35328, 512)
        RCMP = RECIP

        P0 = psum("P0", (128, 1024), F32)
        P1 = psum("P1", (128, 1024), F32)
        P2 = psum("P2", (128, 1024), F32)
        P3 = psum("P3", (128, 512), F32)
        PTB = psum("PTB", (128, 1024), BF16)

        sems = {n: ctx.enter_context(nc.semaphore(n))
                for n in ("st", "sv", "ss", "sg", "sy", "scc")}
        block = ctx.enter_context(nc.Block())

        def op(e, fn, sem=None, amt=1):
            tok = P.inc(sem, amt) if sem else None
            P.q[e].append((fn, sem, amt if sem else 0))
            return tok

        def vop(fn):
            c = P.cnt.get("sv", 0)
            if c:
                P.wait("v", ("sv", c))
            return op("v", fn, "sv")

        def sop(fn):
            c = P.cnt.get("ss", 0)
            if c:
                P.wait("s", ("ss", c))
            return op("s", fn, "ss")

        def top(fn, inc=False):
            return op("t", fn, "st" if inc else None)

        def tcur():
            return ("st", P.cnt["st"])

        def dma(e, fn):
            return op(e, fn, {"g": "sg", "y": "sy", "s": "sy"}[e], 16)
        sems["sq"] = ctx.enter_context(nc.semaphore("sq"))

        def dma(e, fn):
            return op(e, fn, {"g": "sg", "y": "sy", "s": "sq"}[e], 16)

        W = P.wait

        t_iden = dma("y", lambda E: E.dma_start(out=IDEN[:, :], in_=iden_d))
        t_idf = dma("y", lambda E: E.dma_start(out=IDF[:, :], in_=idf_d))
        t_ones = dma("y", lambda E: E.dma_start(out=ONES[:, :], in_=onesq_d))
        t_blk8 = dma("y", lambda E: E.dma_start(out=BLK8[:, :], in_=blk8_d))
        t_cos = dma("y", lambda E: E.dma_start(
            out=COSH[:, :].rearrange("p (c n) -> p c n", c=NT),
            in_=cosh_d.rearrange("(c p) n -> p c n", p=128)))
        t_sinn = dma("y", lambda E: E.dma_start(
            out=SINN[:, :].rearrange("p (c n) -> p c n", c=NT),
            in_=sinn_d.rearrange("(c p) n -> p c n", p=128)))
        t_sinp = dma("y", lambda E: E.dma_start(
            out=SINP[:, :].rearrange("p (c n) -> p c n", c=NT),
            in_=sinp_d.rearrange("(c p) n -> p c n", p=128)))
        t_gate = dma("g", lambda E: E.dma_start(
            out=GATE[:, :].rearrange("p (c n) -> p c n", c=NH),
            in_=gate_d.rearrange("(c p) n -> p c n", p=128)))
        t_wr = dma("g", lambda E: E.dma_start(
            out=WRS[:, :].rearrange("p (c n) -> p c n", c=NH),
            in_=wr_d.rearrange("(c p) n -> p c n", p=128)))
        t_brr = dma("g", lambda E: E.dma_start(out=BRR[:, :], in_=br_d))
        t_x = dma("y", lambda E: E.dma_start(
            out=X[:, :].rearrange("p (c n) -> p c n", c=NT),
            in_=x_d.rearrange("(c p) n -> p c n", p=128)))

        x_wtok = [t_x] * NT

        def emit_ln():
            last = None
            prev_rs = None
            for tc in range(NT):
                Xc = X[:, tc * H:(tc + 1) * H]
                W("v", x_wtok[tc])
                W("v", prev_rs)
                vop(lambda E, Xc=Xc: E.reduce_sum(ST1[:, 0:1], Xc, axis=AX))
                t_sq = vop(lambda E, Xc=Xc: E.tensor_mul(XLN[:, :], Xc, Xc))
                W("v", t_sq)
                t_s1 = vop(lambda E: E.reduce_sum(ST1[:, 1:2], XLN[:, :],
                                                  axis=AX))
                W("v", t_s1)
                vop(lambda E: E.tensor_scalar(ST1[:, 2:3], ST1[:, 0:1],
                                              1.0 / H, None, ALU.mult))
                vop(lambda E: E.tensor_scalar(ST1[:, 3:4], ST1[:, 1:2],
                                              1.0 / H, None, ALU.mult))
                vop(lambda E: E.tensor_mul(ST1[:, 4:5], ST1[:, 2:3],
                                           ST1[:, 2:3]))
                vop(lambda E: E.tensor_sub(ST1[:, 5:6], ST1[:, 3:4],
                                           ST1[:, 4:5]))
                t_vp = vop(lambda E: E.tensor_scalar(ST1[:, 5:6], ST1[:, 5:6],
                                                     1e-5, None, ALU.add))
                W("s", t_vp)
                t_l = sop(lambda E: E.activation(ST1[:, 6:7], ST1[:, 5:6],
                                                 AF.Ln))
                W("s", t_l)
                t_rs = sop(lambda E: E.activation(ST1[:, 6:7], ST1[:, 6:7],
                                                  AF.Exp, scale=-0.5))
                prev_rs = t_rs
                W("v", t_rs)
                t_nm = vop(lambda E: E.scalar_tensor_tensor(
                    ST1[:, 7:8], ST1[:, 2:3], -1.0, ST1[:, 6:7],
                    ALU.mult, ALU.mult))
                W("v", t_nm)
                t_ap = vop(lambda E, Xc=Xc: E.scalar_tensor_tensor(
                    XLN[:, :], Xc, ST1[:, 6:7],
                    ST1[:, 7:8].broadcast_to([128, H]), ALU.mult, ALU.add))
                for half in range(2):
                    W("t", t_ap)
                    W("t", t_iden)
                    tts = None
                    for k in range(8):
                        hc = half * 8 + k
                        tts = top(lambda E, k=k, hc=hc: E.transpose(
                            PTB[:, k * 128:(k + 1) * 128],
                            XLN[:, hc * 128:(hc + 1) * 128], IDEN[:, :]),
                            inc=(k == 7))
                    W("v", tts)
                    t_cp = vop(lambda E, tc=tc, half=half: E.tensor_copy(
                        XT[:, :].rearrange("p (c n) -> p c n", c=NH)[
                            :, half * 8:half * 8 + 8, tc * 128:(tc + 1) * 128],
                        PTB[:, :].rearrange("p (k n) -> p k n", k=8)))
                    W("t", t_cp)
                    last = t_cp
            return last

        GCTR = [0]
        W2CTR = [0]
        A_RT = {}
        B_BT = {}

        def emit_ffn(xt_tok, wguard=None, dbg=False):
            W("t", t_gate)
            W("t", xt_tok)
            lgv = {}
            t11 = None
            for tc in range(NT):
                bank = P2[:, (tc % 2) * 512:(tc % 2) * 512 + 32]
                W("t", lgv.get(tc % 2))
                for hc in range(NH):
                    top(lambda E, b=bank, hc=hc, tc=tc: E.matmul(
                        b, XT[:, hc * TLOC + tc * 128:hc * TLOC + (tc + 1) * 128],
                        GATE[:, hc * 32:(hc + 1) * 32],
                        start=(hc == 0), stop=(hc == NH - 1)),
                        inc=(hc == NH - 1))
                t_mm = tcur()
                W("v", t_mm)
                t_cp = vop(lambda E, b=bank, tc=tc: E.tensor_copy(
                    LOGIT[:, tc * 32:(tc + 1) * 32], b))
                lgv[tc % 2] = t_cp
                Lc = LOGIT[:, tc * 32:(tc + 1) * 32]
                W("v", t_cp)
                vop(lambda E, Lc=Lc: E.reduce_max(GS[:, 0:1], Lc, axis=AX))
                t1 = vop(lambda E, Lc=Lc: E.tensor_scalar(
                    GT1[:, :], Lc, GS[:, 0:1], None, ALU.subtract))
                W("s", t1)
                t2 = sop(lambda E: E.activation(GT2[:, :], GT1[:, :], AF.Exp))
                W("v", t2)
                vop(lambda E: E.reduce_sum(GS[:, 1:2], GT2[:, :], axis=AX))
                vop(lambda E: E.reciprocal(GS[:, 2:3], GS[:, 1:2]))
                t3 = vop(lambda E: E.tensor_scalar(
                    GSC[:, :], GT2[:, :], GS[:, 2:3], None, ALU.mult))
                W("v", t3)
                Sv = GSC[:, :].rearrange("p (a b) -> p a b", b=2)
                vop(lambda E, Sv=Sv: E.tensor_max(
                    GG[:, :].rearrange("p (a b) -> p a b", b=1),
                    Sv[:, :, 0:1], Sv[:, :, 1:2]))
                Gv = GG[:, :].rearrange("p (a b) -> p a b", b=2)
                t4 = vop(lambda E, Gv=Gv: E.tensor_max(
                    GH[:, :].rearrange("p (a b) -> p a b", b=1),
                    Gv[:, :, 0:1], Gv[:, :, 1:2]))
                W("v", t4)
                vop(lambda E: E.reduce_max(GS[:, 3:4], GH[:, :], axis=AX))
                vop(lambda E: E.tensor_scalar(
                    GH2[:, :], GH[:, :], GS[:, 3:4], None, ALU.is_ge))
                vop(lambda E: E.scalar_tensor_tensor(
                    GH2[:, :], GH2[:, :], -1e30, GH[:, :], ALU.mult, ALU.add))
                vop(lambda E: E.reduce_max(GS[:, 3:4], GH2[:, :], axis=AX))
                vop(lambda E: E.tensor_scalar(GT3[:, 0:8], GH2[:, :],
                                              GS[:, 3:4], None, ALU.is_ge))
                vop(lambda E: E.scalar_tensor_tensor(
                    GH2[:, :], GT3[:, 0:8], -1e30, GH2[:, :],
                    ALU.mult, ALU.add))
                t5 = vop(lambda E: E.reduce_max(GS[:, 3:4], GH2[:, :], axis=AX))
                W("v", t5)
                t6 = vop(lambda E: E.tensor_scalar(
                    GH[:, :], GH[:, :], GS[:, 3:4], None, ALU.is_ge))
                W("v", t6)
                for r in range(4):
                    vop(lambda E, r=r: E.tensor_copy(
                        GT1[:, :].rearrange("p (a b) -> p a b", b=4)[
                            :, :, r:r + 1],
                        GH[:, :].rearrange("p (a b) -> p a b", b=1)))
                t7 = vop(lambda E: E.tensor_mul(GT2[:, :], GSC[:, :], GT1[:, :]))
                W("v", t7)
                t8 = vop(lambda E: E.tensor_copy(GT1[:, :], GT2[:, :]))
                W("v", t8)
                for it in range(5):
                    vop(lambda E: E.reduce_max(GS[:, 4:5], GT1[:, :], axis=AX))
                    vop(lambda E: E.tensor_scalar(
                        GT3[:, :], GT1[:, :], GS[:, 4:5], None, ALU.is_ge))
                    t9 = vop(lambda E: E.scalar_tensor_tensor(
                        GT1[:, :], GT3[:, :], -1e30, GT1[:, :],
                        ALU.mult, ALU.add))
                    W("v", t9)
                vop(lambda E: E.reduce_max(GS[:, 4:5], GT1[:, :], axis=AX))
                vop(lambda E: E.tensor_scalar(
                    GT3[:, :], GT2[:, :], GS[:, 4:5], None, ALU.is_ge))
                t10 = vop(lambda E: E.tensor_mul(GT1[:, :], GT2[:, :],
                                                 GT3[:, :]))
                W("v", t10)
                vop(lambda E: E.reduce_sum(GS[:, 5:6], GT1[:, :], axis=AX))
                vop(lambda E: E.tensor_scalar(GS[:, 5:6], GS[:, 5:6], 1e-20,
                                              None, ALU.add))
                vop(lambda E: E.reciprocal(GS[:, 6:7], GS[:, 5:6]))
                t11 = vop(lambda E, tc=tc: E.tensor_scalar(
                    COMB[:, tc * 32:(tc + 1) * 32], GT1[:, :], GS[:, 6:7],
                    0.5, ALU.mult, ALU.mult))
            comb_tok = t11
            if DEBUG and dbg:
                W("y", comb_tok)
                dma("y", lambda E: E.dma_start(out=d_comb, in_=COMB[:, :]))

            a_ev = {}
            b_ev = {}
            ht_rd = [None, None]
            last_a_ev = None
            last_b = None
            for e in range(NEALL):
                par = e % 2
                for ic in range(8):
                    gi = GCTR[0]
                    GCTR[0] += 1
                    buf = W13[gi % 3]
                    if wguard is not None:
                        W("g", wguard)
                    if gi >= 3:
                        W("g", A_RT[gi - 3])
                    tw1 = dma("g", lambda E, e=e, ic=ic, buf=buf: E.dma_start(
                        out=buf[:, 0:2048].rearrange("p (c n) -> p c n", c=NH),
                        in_=w1_d[e][:, ic * 128:(ic + 1) * 128].rearrange(
                            "(c p) n -> p c n", p=128)))
                    tw3 = dma("g", lambda E, e=e, ic=ic, buf=buf: E.dma_start(
                        out=buf[:, 2048:4096].rearrange("p (c n) -> p c n",
                                                        c=NH),
                        in_=w3_d[e][:, ic * 128:(ic + 1) * 128].rearrange(
                            "(c p) n -> p c n", p=128)))
                    for th in range(2):
                        pb0 = P0[:, th * 512:(th + 1) * 512]
                        pb1 = P1[:, th * 512:(th + 1) * 512]
                        W("t", a_ev.get(th))
                        W("t", tw1)
                        W("t", tw3)
                        W("t", xt_tok)
                        if e == 0 and ic == 0:
                            W("t", comb_tok)
                        for hc in range(NH):
                            top(lambda E, pb0=pb0, hc=hc, th=th, buf=buf:
                                E.matmul(
                                    pb0, buf[:, hc * 128:(hc + 1) * 128],
                                    XT[:, hc * TLOC + th * 512:
                                       hc * TLOC + (th + 1) * 512],
                                    start=(hc == 0), stop=(hc == NH - 1)))
                            top(lambda E, pb1=pb1, hc=hc, th=th, buf=buf:
                                E.matmul(
                                    pb1,
                                    buf[:, 2048 + hc * 128:
                                        2048 + (hc + 1) * 128],
                                    XT[:, hc * TLOC + th * 512:
                                       hc * TLOC + (th + 1) * 512],
                                    start=(hc == 0), stop=(hc == NH - 1)),
                                inc=(hc == NH - 1))
                        t_mm = tcur()
                        A_RT[gi] = t_mm
                        W("s", t_mm)
                        W("s", ht_rd[par])
                        W("s", last_a_ev)
                        t_si = sop(lambda E, pb0=pb0: E.activation(
                            SILF[:, :], pb0, AF.Silu))
                        W("v", t_si)
                        W("v", ht_rd[par])
                        t_ev = vop(lambda E, pb1=pb1, par=par, ic=ic, th=th:
                                   E.tensor_mul(
                                       HT[par][:, ic * 1024 + th * 512:
                                               ic * 1024 + (th + 1) * 512],
                                       SILF[:, :], pb1))
                        a_ev[th] = t_ev
                        last_a_ev = t_ev
                        if DEBUG and dbg and e == 0 and ic == 7 and th == 1:
                            W("g", t_ev)
                            dma("g", lambda E: E.dma_start(out=d_ht,
                                                           in_=HT[0][:, :]))
                for hb in range(4):
                    wi = W2CTR[0]
                    W2CTR[0] += 1
                    wbuf = W2B[wi % 2]
                    if wguard is not None:
                        W("g", wguard)
                    if wi >= 2:
                        W("g", B_BT[wi - 2])
                    tw2 = dma("g", lambda E, e=e, hb=hb, wbuf=wbuf: E.dma_start(
                        out=wbuf[:, :].rearrange("p (c n) -> p c n", c=8),
                        in_=w2_d[e][:, hb * 512:(hb + 1) * 512].rearrange(
                            "(c p) n -> p c n", p=128)))
                    for tc in range(NT):
                        bank = P2[:, (tc % 2) * 512:(tc % 2 + 1) * 512]
                        W("t", b_ev.get(tc % 2))
                        W("t", tw2)
                        W("t", last_a_ev)
                        for ic in range(8):
                            top(lambda E, bank=bank, ic=ic, tc=tc, par=par,
                                wbuf=wbuf: E.matmul(
                                    bank,
                                    HT[par][:, ic * 1024 + tc * 128:
                                            ic * 1024 + (tc + 1) * 128],
                                    wbuf[:, ic * 512:(ic + 1) * 512],
                                    start=(ic == 0), stop=(ic == 7)),
                                inc=(ic == 7))
                        t_mm = tcur()
                        last_b = t_mm
                        B_BT[wi] = t_mm
                        W("v", t_mm)
                        W("v", x_wtok[tc])
                        sl = (tc * H + hb * 512, tc * H + (hb + 1) * 512)
                        if e < NE:
                            t_ev = vop(lambda E, bank=bank, tc=tc, e=e, sl=sl:
                                       E.scalar_tensor_tensor(
                                           X[:, sl[0]:sl[1]], bank,
                                           COMB[:, tc * 32 + e:tc * 32 + e + 1],
                                           X[:, sl[0]:sl[1]],
                                           ALU.mult, ALU.add))
                        else:
                            t_ev = vop(lambda E, bank=bank, sl=sl:
                                       E.scalar_tensor_tensor(
                                           X[:, sl[0]:sl[1]], bank, 0.5,
                                           X[:, sl[0]:sl[1]],
                                           ALU.mult, ALU.add))
                        b_ev[tc % 2] = t_ev
                        x_wtok[tc] = t_ev
                ht_rd[par] = last_b
            return tcur()

        def emit_attn(xt_tok, guard):
            ws_rd = [None, None]
            wsctr = [0]
            misc = {}

            def load_ws(wd, bias_d, nb, width):
                i = wsctr[0] % 2
                wsctr[0] += 1
                W("g", guard)
                W("g", ws_rd[i])
                t = dma("g", lambda E, wd=wd, nb=nb, width=width, i=i:
                        E.dma_start(
                            out=WS[i][:, 0:NH * width].rearrange(
                                "p (c n) -> p c n", c=NH),
                            in_=wd[:, nb * width:(nb + 1) * width].rearrange(
                                "(c p) n -> p c n", p=128)))
                tb = dma("g", lambda E, bias_d=bias_d, nb=nb, width=width, i=i:
                         E.dma_start(out=BR1[0:1, i * 256:i * 256 + width],
                                     in_=bias_d[:, nb * width:(nb + 1) * width]))
                return i, t, tb

            kn_init = vop(lambda E: E.memset(KN2L[:, :], 0.0))
            pb_cycle = [P0[:, 0:256], P0[:, 512:768], P1[:, 0:256],
                        P1[:, 512:768]]
            pb_ev = [None] * 4
            ccin_toks = []
            vstg_rd = [None]
            kstg_rd = [None]

            def proj_block(wd, bias_d, nb, width, kind):
                i, tw, tb = load_ws(wd, bias_d, nb, width)
                last_mm = None
                for tc in range(NT):
                    pbank = pb_cycle[tc % 4][:, 0:width]
                    W("t", pb_ev[tc % 4])
                    W("t", tw)
                    W("t", tb)
                    W("t", xt_tok)
                    for hc in range(NH):
                        top(lambda E, pbank=pbank, hc=hc, tc=tc, i=i,
                            width=width: E.matmul(
                                pbank,
                                XT[:, hc * TLOC + tc * 128:
                                   hc * TLOC + (tc + 1) * 128],
                                WS[i][:, hc * width:(hc + 1) * width],
                                start=(hc == 0), stop=False))
                    top(lambda E, pbank=pbank, i=i, width=width: E.matmul(
                        pbank, ONES[0:1, 0:128], BR1[0:1, i * 256:i * 256 + width],
                        start=False, stop=True), inc=True)
                    t_mm = tcur()
                    last_mm = t_mm
                    if kind == "v":
                        W("s", t_mm)
                        W("s", vstg_rd[0])
                        t_cv = sop(lambda E, pbank=pbank, width=width: E.copy(
                            VSTG[:, 0:width], pbank))
                        pb_ev[tc % 4] = t_cv
                        W("y", t_cv)
                        W("y", guard)
                        t_d = dma("y", lambda E, tc=tc, nb=nb, width=width:
                                  E.dma_start(
                                      out=ccin.ap()[
                                          1024 + tc * 128:1024 + (tc + 1) * 128,
                                          nb * width:(nb + 1) * width],
                                      in_=VSTG[:, 0:width]))
                        vstg_rd[0] = t_d
                        ccin_toks.append(t_d)
                        continue
                    W("s", t_mm)
                    W("s", misc.get("last_rp"))
                    t_cp = sop(lambda E, pbank=pbank, width=width: E.copy(
                        SILF[:, 0:width], pbank))
                    pb_ev[tc % 4] = t_cp
                    W("v", t_cp)
                    W("v", t_cos)
                    W("v", t_sinn)
                    W("v", t_sinp)
                    hh = width // 128
                    for j in range(hh):
                        o = j * 128
                        vop(lambda E, o=o, tc=tc: E.tensor_mul(
                            ROPA[:, o:o + 128].rearrange(
                                "p (a n) -> p a n", n=64),
                            SILF[:, o:o + 128].rearrange(
                                "p (a n) -> p a n", n=64),
                            COSH[:, tc * 64:(tc + 1) * 64].rearrange(
                                "p (a n) -> p a n", a=1).broadcast_to(
                                    [128, 2, 64])))
                        vop(lambda E, o=o, tc=tc: E.tensor_mul(
                            ROPB[:, o:o + 64], SILF[:, o + 64:o + 128],
                            SINN[:, tc * 64:(tc + 1) * 64]))
                        vop(lambda E, o=o, tc=tc: E.tensor_mul(
                            ROPB[:, o + 64:o + 128], SILF[:, o:o + 64],
                            SINP[:, tc * 64:(tc + 1) * 64]))
                    t_rp = vop(lambda E, width=width: E.tensor_add(
                        XLN[:, 0:width], ROPA[:, 0:width], ROPB[:, 0:width]))
                    misc["last_rp"] = t_rp
                    if kind == "k":
                        W("v", t_rp)
                        W("v", kn_init)
                        t_q2 = vop(lambda E, width=width: E.tensor_mul(
                            KSQ2[:, 0:width], XLN[:, 0:width], XLN[:, 0:width]))
                        W("v", t_q2)
                        for j in range(hh):
                            vop(lambda E, j=j: E.reduce_sum(
                                KN2S[:, j:j + 1],
                                KSQ2[:, j * 128:(j + 1) * 128], axis=AX))
                        vop(lambda E, nb=nb, hh=hh: E.tensor_max(
                            KN2L[:, nb * hh:nb * hh + hh],
                            KN2L[:, nb * hh:nb * hh + hh], KN2S[:, 0:hh]))
                    W("t", t_rp)
                    tts = None
                    for j in range(hh):
                        tts = top(lambda E, j=j: E.transpose(
                            PTB[:, j * 128:(j + 1) * 128],
                            XLN[:, j * 128:(j + 1) * 128], IDEN[:, :]),
                            inc=(j == hh - 1))
                    W("v", tts)
                    if kind == "q":
                        t_c2 = vop(lambda E, nb=nb, tc=tc, hh=hh: E.tensor_copy(
                            QT[:, :].rearrange("p (h n) -> p h n", h=HEADS)[
                                :, nb * hh:nb * hh + hh,
                                tc * 128:(tc + 1) * 128],
                            PTB[:, 0:hh * 128].rearrange(
                                "p (k n) -> p k n", k=hh)))
                    else:
                        W("v", kstg_rd[0])
                        t_c2 = vop(lambda E, tc=tc, hh=hh: E.tensor_copy(
                            KSTG[:, :].rearrange("p (h n) -> p h n", h=2)[
                                :, 0:hh, tc * 128:(tc + 1) * 128],
                            PTB[:, 0:hh * 128].rearrange(
                                "p (k n) -> p k n", k=hh)))
                    W("t", t_c2)
                    misc["last_c2"] = t_c2
                if kind == "k":
                    for j in range(2):
                        head = nb * 2 + j
                        W("y", misc["last_c2"])
                        W("y", guard)
                        t_d = dma("y", lambda E, head=head, j=j: E.dma_start(
                            out=ccin.ap()[head * 128:(head + 1) * 128, :],
                            in_=KSTG[:, j * 1024:(j + 1) * 1024]))
                        kstg_rd[0] = t_d
                        ccin_toks.append(t_d)
                ws_rd[i] = last_mm
                return last_mm

            for nb in range(4):
                proj_block(wk_d, bk_d, nb, 256, "k")
            for nb in range(4):
                proj_block(wv_d, bv_d, nb, 256, "v")

            W("t", t_wr)
            W("t", t_brr)
            r_last = None
            rs_rd = None
            for tc in range(NT):
                W("t", r_last)
                for hc in range(NH):
                    top(lambda E, hc=hc, tc=tc: E.matmul(
                        P3[:, 0:16],
                        XT[:, hc * TLOC + tc * 128:hc * TLOC + (tc + 1) * 128],
                        WRS[:, hc * 16:(hc + 1) * 16],
                        start=(hc == 0), stop=False))
                top(lambda E: E.matmul(
                    P3[:, 0:16], ONES[0:1, 0:128], BRR[0:1, 0:16],
                    start=False, stop=True), inc=True)
                t_mm = tcur()
                W("v", t_mm)
                r_last = vop(lambda E, tc=tc: E.tensor_copy(
                    RLOG[:, tc * 16:(tc + 1) * 16], P3[:, 0:16]))
            rt_last = None
            for tc in range(NT):
                W("t", r_last)
                W("t", rt_last)
                W("t", t_idf)
                t_tr = top(lambda E, tc=tc: E.transpose(
                    P2[0:16, 0:128], RLOG[:, tc * 16:(tc + 1) * 16],
                    IDF[:, :]), inc=True)
                W("v", t_tr)
                W("v", rs_rd)
                t_cpr = vop(lambda E: E.tensor_copy(RSM[:, :],
                                                    P2[0:16, 0:128]))
                rt_last = t_cpr
                W("y", t_cpr)
                t_d = dma("y", lambda E, tc=tc: E.dma_start(
                    out=ccin.ap()[2048:2080, :].bitcast(F32).rearrange(
                        "(r two) n -> r (two n)", two=2)[
                            :, tc * 128:(tc + 1) * 128],
                    in_=RSM[:, :]))
                rs_rd = t_d
                ccin_toks.append(t_d)
            W("t", rt_last)
            t_kt = top(lambda E: E.transpose(P2[0:8, 0:128], KN2L[:, :],
                                             IDF[:, :]), inc=True)
            W("v", t_kt)
            W("v", rs_rd)
            t_kc = vop(lambda E: E.tensor_copy(RSM[0:8, 0:128],
                                               P2[0:8, 0:128]))
            W("v", t_kc)
            t_kr = vop(lambda E: E.reduce_max(KN2S[0:8, 0:1],
                                              RSM[0:8, 0:128], axis=AX))
            W("s", t_kr)
            t_kra = dma("s", lambda E: E.dma_start(
                out=krowd.ap(), in_=KN2S[0:8, 0:1]))
            W("s", t_kra)
            t_krd = dma("s", lambda E: E.dma_start(
                out=KROW[0:1, 0:8],
                in_=krowd.ap().rearrange("(o p) n -> o (p n)", o=1)))
            W("y", t_krd)
            t_d = dma("y", lambda E: E.dma_start(
                out=ccin.ap()[2080:2081, :].bitcast(F32)[:, 0:8],
                in_=KROW[0:1, 0:8]))
            ccin_toks.append(t_d)

            for t in ccin_toks:
                W("g", t)
            cc_tok = op("g", lambda E: E.collective_compute(
                "AllGather", ALU.bypass,
                replica_groups=[[0, 1, 2, 3], [4, 5, 6, 7]],
                ins=[ccin.ap()], outs=[ccout.ap()]), "scc", 1)

            for nb in range(8):
                proj_block(wq_d, bq_d, nb, 256, "q")
            qt_tok = misc["last_c2"]
            for i in range(4):
                W("t", pb_ev[i])

            W("y", cc_tok)
            t_rl = None
            for r in range(4):
                t_rl = dma("y", lambda E, r=r: E.dma_start(
                    out=RL[r * 32:(r + 1) * 32, :],
                    in_=ccout.ap().bitcast(F32)[
                        r * CCR + 2048:r * CCR + 2080, :]))
            t_kg = None
            for r in range(4):
                t_kg = dma("y", lambda E, r=r: E.dma_start(
                    out=KN2G[0:1, :].rearrange(
                        "o (k r) -> o k r", r=4)[:, :, r:r + 1],
                    in_=ccout.ap().bitcast(F32)[
                        r * CCR + 2080:r * CCR + 2081, 0:8],
                    allow_slow_non_contiguous=True))
            W("v", t_kg)
            t_kx = vop(lambda E: E.reduce_max(
                KMAX2[0:1, :],
                KN2G[0:1, :].rearrange("o (a b) -> o a b", b=4), axis=AX))
            W("v", t_kx)
            vop(lambda E: E.tensor_scalar(CC12[0:1, 0:8], KMAX2[0:1, :],
                                          1.0 / DH, None, ALU.mult))
            t_c = vop(lambda E: E.tensor_scalar(CC12[0:1, 8:16], KMAX2[0:1, :],
                                                float(DH), None, ALU.mult))
            W("s", t_c)
            t_sq = sop(lambda E: E.activation(CC12[0:1, 16:32],
                                              CC12[0:1, 0:16], AF.Sqrt))
            W("v", t_sq)
            t_c12 = vop(lambda E: E.tensor_scalar(
                CC12[0:1, 0:16], CC12[0:1, 16:32], -0.3, None, ALU.mult))

            W("v", t_rl)
            vop(lambda E: E.memset(RTHR[:, 0:1], 0.0))
            t_bi = vop(lambda E: E.memset(RTHR[:, 1:2], 0.0))
            cnt_tok = t_bi
            cm_rd = None
            for it in range(33):
                W("v", cnt_tok)
                W("v", cm_rd)
                vop(lambda E: E.tensor_scalar(RCMP[:, :], RL[:, :],
                                              RTHR[:, 0:1], None, ALU.is_ge))
                t_r = vop(lambda E: E.reduce_sum(KN2S[:, 2:3], RCMP[:, :],
                                                 axis=AX))
                W("t", t_r)
                W("t", t_blk8)
                W("t", cm_rd)
                top(lambda E: E.matmul(P3[:, 0:1], BLK8[:, :], KN2S[:, 2:3],
                                       start=True, stop=True), inc=True)
                t_cm = tcur()
                W("v", t_cm)
                t_dir = vop(lambda E: E.tensor_scalar(
                    KN2S[:, 3:4], P3[:, 0:1], float(TOPK), None, ALU.is_gt))
                cm_rd = t_dir
                delta = 64.0 * (0.5 ** it)
                vop(lambda E, delta=delta: E.tensor_scalar(
                    KN2S[:, 4:5], KN2S[:, 3:4], 2.0 * delta, delta,
                    ALU.mult, ALU.subtract))
                cnt_tok = vop(lambda E: E.tensor_add(
                    RTHR[:, 0:1], RTHR[:, 0:1], KN2S[:, 4:5]))
            W("v", cnt_tok)
            if DEBUG:
                W("y", cnt_tok)
                dma("y", lambda E: E.dma_start(out=d_rl, in_=RL[:, :]))
                dma("y", lambda E: E.dma_start(out=d_thr, in_=RTHR[:, :]))
                W("y", t_c12)
                dma("y", lambda E: E.dma_start(out=d_kx, in_=KN2G[0:1, :]))
            t_mk = vop(lambda E: E.tensor_scalar(
                MASKB[:, :], RL[:, :], RTHR[:, 0:1], -1e9,
                ALU.is_lt, ALU.mult))
            W("y", t_mk)
            t_md = dma("y", lambda E: E.dma_start(out=masknd.ap()[:, :],
                                                  in_=MASKB[:, :]))
            t_on = dma("s", lambda E: E.dma_start(out=RHS2[0:1, :],
                                                  in_=onesrow_d[:, 0:TLOC]))

            ktv_rd = None
            kv_tok = None
            st_ev = [None, None]
            out_ev = [None, None]
            den_ev = cm_rd
            pte_rd = [None, None]
            b2_rd = None
            nm_tok = None
            att_toks = []
            t_av = None
            for head in range(HEADS):
                kv = head // 2
                if head % 2 == 0:
                    W("y", ktv_rd)
                    W("y", cc_tok)
                    W("y", guard)
                    for r in range(4):
                        dma("y", lambda E, r=r, kv=kv: E.dma_start(
                            out=KTKV[:, r * 1024:(r + 1) * 1024],
                            in_=ccout.ap()[r * CCR + kv * 128:
                                           r * CCR + (kv + 1) * 128, :]))
                        kv_tok = dma("y", lambda E, r=r, kv=kv: E.dma_start(
                            out=VKV[:, r * 1024:(r + 1) * 1024].rearrange(
                                "p (lc d) -> p lc d", d=128),
                            in_=ccout.ap()[r * CCR + 1024:r * CCR + 2048,
                                           kv * 128:(kv + 1) * 128].rearrange(
                                               "(lc p) d -> p lc d", p=128)))
                W("s", b2_rd)
                W("s", t_md)
                b2a = dma("s", lambda E, head=head: E.dma_start(
                    out=BIAS2[0:1, :].rearrange(
                        "o (r g n) -> o r g n", r=4, g=2),
                    in_=masknd.ap().rearrange(
                        "(r g) n -> r g n", g=32)[
                            :, 2 * head:2 * head + 2, :]))
                b2b = dma("s", lambda E: E.dma_start(
                    out=BIAS2[1:2, :], in_=onesrow_d[:, :]))
                W("v", qt_tok)
                W("v", nm_tok)
                t_qs = vop(lambda E, head=head: E.tensor_mul(
                    QSQ[:, :], QT[:, head * 1024:(head + 1) * 1024],
                    QT[:, head * 1024:(head + 1) * 1024]))
                W("t", t_qs)
                W("t", den_ev)
                t_qn = None
                for qh in range(2):
                    t_qn = top(lambda E, qh=qh: E.matmul(
                        P1[0:1, qh * 512:(qh + 1) * 512], ONES[:, 0:1],
                        QSQ[:, qh * 512:(qh + 1) * 512],
                        start=True, stop=True), inc=(qh == 1))
                W("v", t_qn)
                W("v", t_c12)
                nm_tok = vop(lambda E, kv=kv: E.scalar_tensor_tensor(
                    NMB[0:1, :], P1[0:1, 0:1024], CC12[0:1, kv:kv + 1],
                    CC12[0:1, 8 + kv:9 + kv].broadcast_to([1, TLOC]),
                    ALU.mult, ALU.add))
                W("s", nm_tok)
                t_r2 = dma("s", lambda E: E.dma_start(out=RHS2[1:2, :],
                                                      in_=NMB[0:1, :]))
                for qh in range(2):
                    op2 = P2[:, qh * 512:(qh + 1) * 512]
                    for c in range(32):
                        pp = c % 2
                        stb = P0[:, pp * 512:(pp + 1) * 512]
                        W("t", st_ev[pp])
                        W("t", kv_tok)
                        W("t", b2a)
                        W("t", b2b)
                        W("t", t_r2)
                        W("t", t_on)
                        if c == 0:
                            W("t", out_ev[qh])
                            W("t", den_ev)
                        top(lambda E, stb=stb, c=c, head=head, qh=qh:
                            E.matmul(
                                stb, KTKV[:, c * 128:(c + 1) * 128],
                                QT[:, head * 1024 + qh * 512:
                                   head * 1024 + (qh + 1) * 512],
                                start=True, stop=False))
                        t_st = top(lambda E, stb=stb, c=c, qh=qh: E.matmul(
                            stb, BIAS2[:, c * 128:(c + 1) * 128],
                            RHS2[:, qh * 512:(qh + 1) * 512],
                            start=False, stop=True), inc=True)
                        W("s", t_st)
                        W("s", pte_rd[pp])
                        t_ex = sop(lambda E, stb=stb, pp=pp: E.activation(
                            PTE[:, pp * 512:(pp + 1) * 512], stb, AF.Exp,
                            scale=INV_SQRT_D))
                        st_ev[pp] = t_ex
                        W("t", t_ex)
                        top(lambda E, op2=op2, c=c, pp=pp: E.matmul(
                            op2, VKV[:, c * 128:(c + 1) * 128],
                            PTE[:, pp * 512:(pp + 1) * 512],
                            start=(c == 0), stop=(c == 31)))
                        t_av = top(lambda E, c=c, pp=pp: E.matmul(
                            P3[:, :], ONES[:, :],
                            PTE[:, pp * 512:(pp + 1) * 512],
                            start=(c == 0), stop=(c == 31)), inc=True)
                        pte_rd[pp] = t_av
                    W("v", t_av)
                    t_rc = vop(lambda E: E.reciprocal(RECIP[:, :], P3[:, :]))
                    den_ev = t_rc
                    W("v", t_rc)
                    t_no = vop(lambda E, op2=op2, head=head, qh=qh:
                               E.tensor_mul(
                                   XT[:, head * 1024 + qh * 512:
                                      head * 1024 + (qh + 1) * 512],
                                   op2, RECIP[:, :]))
                    out_ev[qh] = t_no
                    att_toks.append(t_no)
                ktv_rd = t_av
                b2_rd = t_av
            at_last = att_toks[-1]
            if DEBUG:
                W("g", at_last)
                W("g", qt_tok)
                dma("g", lambda E: E.dma_start(out=d_att, in_=XT[:, :]))
                dma("g", lambda E: E.dma_start(out=d_qt, in_=QT[:, :]))
                dma("g", lambda E: E.dma_start(out=d_kt, in_=KTKV[:, :]))

            W("t", st_ev[0])
            W("t", st_ev[1])
            o_ev = [None, None]
            o_last = None
            for nb in range(8):
                i, tw, tb = load_ws(wo_d, bo_d, nb, 256)
                for tc in range(NT):
                    pb = [P0[:, 0:256], P0[:, 512:768]][tc % 2]
                    W("t", o_ev[tc % 2])
                    W("t", tw)
                    W("t", tb)
                    W("t", at_last)
                    for hc in range(NH):
                        top(lambda E, pb=pb, hc=hc, tc=tc, i=i: E.matmul(
                            pb, XT[:, hc * TLOC + tc * 128:
                                   hc * TLOC + (tc + 1) * 128],
                            WS[i][:, hc * 256:(hc + 1) * 256],
                            start=(hc == 0), stop=False))
                    top(lambda E, pb=pb, i=i: E.matmul(
                        pb, ONES[0:1, 0:128], BR1[0:1, i * 256:i * 256 + 256],
                        start=False, stop=True), inc=True)
                    t_mm = tcur()
                    o_last = t_mm
                    W("v", t_mm)
                    W("v", x_wtok[tc])
                    t_ev = vop(lambda E, pb=pb, tc=tc, nb=nb: E.tensor_add(
                        X[:, tc * H + nb * 256:tc * H + (nb + 1) * 256],
                        X[:, tc * H + nb * 256:tc * H + (nb + 1) * 256], pb))
                    o_ev[tc % 2] = t_ev
                    x_wtok[tc] = t_ev
                ws_rd[i] = o_last
            return o_last

        xt1 = emit_ln()
        if DEBUG:
            W("g", xt1)
            dma("g", lambda E: E.dma_start(out=d_xt, in_=XT[:, :]))
        ffn1_end = emit_ffn(xt1, dbg=True)
        if DEBUG:
            for tc in range(NT):
                W("y", x_wtok[tc])
            dma("y", lambda E: E.dma_start(
                out=d_x1.rearrange("(c p) n -> p c n", p=128),
                in_=X[:, :].rearrange("p (c n) -> p c n", c=NT)))
        xt2 = emit_ln()
        o_last = emit_attn(xt2, ffn1_end)
        if DEBUG:
            for tc in range(NT):
                W("y", x_wtok[tc])
            dma("y", lambda E: E.dma_start(
                out=d_x2.rearrange("(c p) n -> p c n", p=128),
                in_=X[:, :].rearrange("p (c n) -> p c n", c=NT)))
        xt3 = emit_ln()
        emit_ffn(xt3, wguard=o_last)
        for tc in range(NT):
            W("y", x_wtok[tc])
        t_out = dma("y", lambda E: E.dma_start(
            out=out_d.rearrange("(c p) n -> p c n", p=128),
            in_=X[:, :].rearrange("p (c n) -> p c n", c=NT)))
        W("y", t_out)

        def runner(key):
            def run(E):
                for item in P.q[key]:
                    if item[0] == "WAIT":
                        E.wait_ge(sems[item[1]], item[2])
                    else:
                        fn, semname, amt = item
                        ins = fn(E)
                        if semname:
                            ins.then_inc(sems[semname], amt)
            return run

        block.tensor(runner("t"))
        block.vector(runner("v"))
        block.scalar(runner("s"))
        block.gpsimd(runner("g"))
        block.sync(runner("y"))

    return nc


_CACHE = {}


def kernel(**inputs):
    x = np.asarray(inputs["L0_attn_hidden_states"])
    w1 = np.asarray(inputs["L0_ffn_w1"])
    w2 = np.asarray(inputs["L0_ffn_w2"])
    w3 = np.asarray(inputs["L0_ffn_w3"])
    sw1 = np.asarray(inputs["L0_ffn_sw1"])
    sw2 = np.asarray(inputs["L0_ffn_sw2"])
    sw3 = np.asarray(inputs["L0_ffn_sw3"])
    gate = np.asarray(inputs["L0_ffn_gate_weight"])

    w1t = np.concatenate([np.ascontiguousarray(w1.transpose(0, 2, 1)),
                          sw1.T.reshape(H, 2, IEXP).transpose(1, 0, 2)], 0)
    w3t = np.concatenate([np.ascontiguousarray(w3.transpose(0, 2, 1)),
                          sw3.T.reshape(H, 2, IEXP).transpose(1, 0, 2)], 0)
    w2t = np.concatenate([np.ascontiguousarray(w2.transpose(0, 2, 1)),
                          np.ascontiguousarray(sw2.T).reshape(2, IEXP, H)], 0)
    w1t = np.ascontiguousarray(w1t)
    w3t = np.ascontiguousarray(w3t)
    w2t = np.ascontiguousarray(w2t)
    gatet = np.ascontiguousarray(gate.T)

    pos = np.arange(SEQ, dtype=np.float64)
    invf = 1.0 / (1e6 ** (np.arange(0, DH, 2, dtype=np.float64) / DH))
    ang = pos[:, None] * invf[None, :]
    cosf = np.cos(ang).astype(np.float32)
    sinf = np.sin(ang).astype(np.float32)

    iden = np.eye(128).astype(BF)
    idf32 = np.eye(128).astype(np.float32)
    onesq = np.ones((128, 128), dtype=BF)
    onesrow = np.ones((1, SEQ), dtype=BF)
    blk8 = np.zeros((128, 128), np.float32)
    hid = (np.arange(128) % 32) // 2
    blk8[hid[:, None] == hid[None, :]] = 1.0

    if "nc" not in _CACHE:
        _CACHE["nc"] = build_nc()
    nc = _CACHE["nc"]

    common = dict(
        w1t=w1t, w3t=w3t, w2t=w2t, gatet=gatet,
        wq=np.asarray(inputs["L0_attn_Wq"]),
        wk=np.asarray(inputs["L0_attn_Wk"]),
        wv=np.asarray(inputs["L0_attn_Wv"]),
        wo=np.asarray(inputs["L0_attn_Wo"]),
        wr=np.asarray(inputs["L0_attn_Wr"]),
        bq=np.ascontiguousarray(np.asarray(inputs["L0_attn_bq"])[None, :]),
        bk=np.ascontiguousarray(np.asarray(inputs["L0_attn_bk"])[None, :]),
        bv=np.ascontiguousarray(np.asarray(inputs["L0_attn_bv"])[None, :]),
        bo=np.ascontiguousarray(np.asarray(inputs["L0_attn_bo"])[None, :]),
        br=np.ascontiguousarray(np.asarray(inputs["L0_attn_br"])[None, :]),
        iden=iden, idf32=idf32, onesq=onesq, onesrow=onesrow, blk8=blk8,
    )
    in_maps = []
    for c in range(8):
        b, r = c // 4, c % 4
        m = dict(common)
        m["x"] = np.ascontiguousarray(x[b, r * TLOC:(r + 1) * TLOC, :])
        sl = slice(r * TLOC, (r + 1) * TLOC)
        m["cosh"] = np.ascontiguousarray(cosf[sl])
        m["sinn"] = np.ascontiguousarray(-sinf[sl])
        m["sinp"] = np.ascontiguousarray(sinf[sl])
        in_maps.append(m)

    res = run_bass_kernel_spmd(nc, in_maps, core_ids=list(range(8)))
    out = np.empty((B, SEQ, H), np.float32)
    for c in range(8):
        b, r = c // 4, c % 4
        out[b, r * TLOC:(r + 1) * TLOC, :] = res.results[c]["out"]
    return out


# version 03
# speedup vs baseline: 1.0007x; 1.0007x over previous
import math
import os
import contextlib
DEBUG = bool(os.environ.get("KMK_DEBUG"))
import numpy as np
import ml_dtypes
import concourse.bass as bass
import concourse.mybir as mybir
from concourse.bass_utils import run_bass_kernel_spmd

F32 = mybir.dt.float32
BF16 = mybir.dt.bfloat16
AF = mybir.ActivationFunctionType
ALU = mybir.AluOpType
AX = mybir.AxisListType.X

B, SEQ, H = 2, 4096, 2048
TLOC, NT, NH = 1024, 8, 16
HEADS, KVH, DH = 16, 8, 128
NE, NEALL, IEXP = 32, 34, 1024
TOPK = 1638
INV_SQRT_D = 1.0 / math.sqrt(DH)
BF = ml_dtypes.bfloat16
CCR = 2081


class Pr:
    ENG = ("t", "v", "s", "g", "y")

    def __init__(self):
        self.q = {e: [] for e in self.ENG}
        self.cnt = {}
        self.lastw = {}

    def inc(self, sem, amt):
        self.cnt[sem] = self.cnt.get(sem, 0) + amt
        return (sem, self.cnt[sem])

    def wait(self, e, tok):
        if tok is None:
            return
        sem, val = tok
        assert val <= self.cnt.get(sem, 0), f"wait on future {sem} {val}"
        key = (e, sem)
        if self.lastw.get(key, -1) >= val:
            return
        self.lastw[key] = val
        self.q[e].append(("WAIT", sem, val))


def build_nc():
    nc = bass.Bass("TRN2", target_bir_lowering=False)
    P = Pr()

    def din(name, shape, dt=F32):
        return nc.dram_tensor(name, list(shape), dt, kind="ExternalInput").ap()

    x_d = din("x", (TLOC, H))
    w1_d = din("w1t", (NEALL, H, IEXP))
    w3_d = din("w3t", (NEALL, H, IEXP))
    w2_d = din("w2t", (NEALL, IEXP, H))
    gate_d = din("gatet", (H, NE))
    wq_d = din("wq", (H, H))
    wk_d = din("wk", (H, KVH * DH))
    wv_d = din("wv", (H, KVH * DH))
    wo_d = din("wo", (H, H))
    wr_d = din("wr", (H, HEADS))
    bq_d = din("bq", (1, H))
    bk_d = din("bk", (1, KVH * DH))
    bv_d = din("bv", (1, KVH * DH))
    bo_d = din("bo", (1, H))
    br_d = din("br", (1, HEADS))
    cosh_d = din("cosh", (TLOC, 64))
    sinn_d = din("sinn", (TLOC, 64))
    sinp_d = din("sinp", (TLOC, 64))
    iden_d = din("iden", (128, 128), BF16)
    idf_d = din("idf32", (128, 128), F32)
    onesq_d = din("onesq", (128, 128), BF16)
    onesrow_d = din("onesrow", (1, SEQ), BF16)
    blk8_d = din("blk8", (128, 128), F32)
    out_d = nc.dram_tensor("out", [TLOC, H], F32, kind="ExternalOutput").ap()
    if DEBUG:
        d_x1 = nc.dram_tensor("d_x1", [TLOC, H], F32, kind="ExternalOutput").ap()
        d_rl = nc.dram_tensor("d_rl", [128, 512], F32, kind="ExternalOutput").ap()
        d_thr = nc.dram_tensor("d_thr", [128, 8], F32, kind="ExternalOutput").ap()
        d_kx = nc.dram_tensor("d_kx", [1, 32], F32, kind="ExternalOutput").ap()
        d_att = nc.dram_tensor("d_att", [128, NH * TLOC], F32, kind="ExternalOutput").ap()
        d_qt = nc.dram_tensor("d_qt", [128, NH * TLOC], F32, kind="ExternalOutput").ap()
        d_kt = nc.dram_tensor("d_kt", [128, 4096], F32, kind="ExternalOutput").ap()
        d_x2 = nc.dram_tensor("d_x2", [TLOC, H], F32, kind="ExternalOutput").ap()
        d_xt = nc.dram_tensor("d_xt", [128, NH * TLOC], F32, kind="ExternalOutput").ap()
        d_comb = nc.dram_tensor("d_comb", [128, NT * 32], F32, kind="ExternalOutput").ap()
        d_ht = nc.dram_tensor("d_ht", [128, 8192], F32, kind="ExternalOutput").ap()

    ccin = nc.dram_tensor("ccin", [CCR, TLOC], BF16, kind="Internal")
    ccout = nc.dram_tensor("ccout", [4 * CCR, TLOC], BF16, kind="Internal")
    masknd = nc.dram_tensor("masknd", [128, 512], BF16, kind="Internal")
    krowd = nc.dram_tensor("krowd", [8, 1], F32, kind="Internal")

    ctx = contextlib.ExitStack()
    sbuf = lambda n, shp, dt: ctx.enter_context(nc.sbuf_tensor(n, list(shp), dt))
    psum = lambda n, shp, dt: ctx.enter_context(nc.psum_tensor(n, list(shp), dt))

    with ctx:
        X = sbuf("X", (128, NT * H), F32)
        XT = sbuf("XT", (128, NH * TLOC), BF16)
        U = sbuf("U", (128, 36864), BF16)
        XLN = sbuf("XLN", (128, H), BF16)
        COSH = sbuf("COSH", (128, NT * 64), F32)
        SINN = sbuf("SINN", (128, NT * 64), F32)
        SINP = sbuf("SINP", (128, NT * 64), F32)
        SILF = sbuf("SILF", (128, 512), F32)
        ROPA = sbuf("ROPA", (128, 256), F32)
        ROPB = sbuf("ROPB", (128, 256), F32)
        LOGIT = sbuf("LOGIT", (128, NT * 32), F32)
        GSC = sbuf("GSC", (128, 32), F32)
        GT1 = sbuf("GT1", (128, 32), F32)
        GT2 = sbuf("GT2", (128, 32), F32)
        GT3 = sbuf("GT3", (128, 32), F32)
        GG = sbuf("GG", (128, 16), F32)
        GH = sbuf("GH", (128, 8), F32)
        GH2 = sbuf("GH2", (128, 8), F32)
        GS = sbuf("GS", (128, 8), F32)
        COMB = sbuf("COMB", (128, NT * 32), F32)
        ST1 = sbuf("ST1", (128, 8), F32)
        RL = sbuf("RL", (128, 512), F32)
        RTHR = sbuf("RTHR", (128, 8), F32)
        RECIP = sbuf("RECIP", (128, 512), F32)
        RLOG = sbuf("RLOG", (128, NT * 16), F32)
        RSM = sbuf("RSM", (16, 128), F32)
        KN2L = sbuf("KN2L", (128, 8), F32)
        KN2S = sbuf("KN2S", (128, 8), F32)
        KROW = sbuf("KROW", (1, 8), F32)
        KN2G = sbuf("KN2G", (1, 32), F32)
        KMAX2 = sbuf("KMAX2", (1, 8), F32)
        CC12 = sbuf("CC12", (1, 32), F32)
        BLK8 = sbuf("BLK8", (128, 128), F32)
        BIAS2 = sbuf("BIAS2", (2, SEQ), BF16)
        RHS2 = sbuf("RHS2", (2, TLOC), BF16)
        NMB = sbuf("NMB", (1, TLOC), BF16)
        ONES = sbuf("ONES", (128, 128), BF16)
        IDEN = sbuf("IDEN", (128, 128), BF16)
        IDF = sbuf("IDF", (128, 128), F32)
        GATE = sbuf("GATE", (128, NH * 32), BF16)
        WRS = sbuf("WRS", (128, NH * 16), BF16)
        BR1 = sbuf("BR1", (1, 512), BF16)
        BRR = sbuf("BRR", (1, 16), BF16)

        def uv(off, n):
            return U[:, off:off + n]
        W13 = [uv(4096 * i, 4096) for i in range(3)]
        W2B = [uv(12288 + 4096 * i, 4096) for i in range(2)]
        HT = [uv(20480 + 8192 * i, 8192) for i in range(2)]
        QT = uv(0, 16384)
        KTKV = uv(16384, 4096)
        VKV = uv(20480, 4096)
        WS = [uv(24576 + 4096 * i, 4096) for i in range(2)]
        KSTG = uv(32768, 2048)
        VSTG = uv(34816, 256)
        QSQ = uv(32768, 1024)
        KSQ2 = uv(35072, 256)
        PTE = uv(33792, 1024)
        MASKB = uv(35328, 512)
        RCMP = RECIP

        P0 = psum("P0", (128, 1024), F32)
        P1 = psum("P1", (128, 1024), F32)
        P2 = psum("P2", (128, 1024), F32)
        P3 = psum("P3", (128, 512), F32)
        PTB = psum("PTB", (128, 1024), BF16)

        sems = {n: ctx.enter_context(nc.semaphore(n))
                for n in ("st", "sv", "ss", "sg", "sy", "scc")}
        block = ctx.enter_context(nc.Block())

        def op(e, fn, sem=None, amt=1):
            tok = P.inc(sem, amt) if sem else None
            P.q[e].append((fn, sem, amt if sem else 0))
            return tok

        def vop(fn):
            c = P.cnt.get("sv", 0)
            if c:
                P.wait("v", ("sv", c))
            return op("v", fn, "sv")

        def sop(fn):
            c = P.cnt.get("ss", 0)
            if c:
                P.wait("s", ("ss", c))
            return op("s", fn, "ss")

        def top(fn, inc=False):
            return op("t", fn, "st" if inc else None)

        def tcur():
            return ("st", P.cnt["st"])

        def dma(e, fn):
            return op(e, fn, {"g": "sg", "y": "sy", "s": "sy"}[e], 16)
        sems["sq"] = ctx.enter_context(nc.semaphore("sq"))

        def dma(e, fn):
            return op(e, fn, {"g": "sg", "y": "sy", "s": "sq"}[e], 16)

        W = P.wait

        t_iden = dma("y", lambda E: E.dma_start(out=IDEN[:, :], in_=iden_d))
        t_idf = dma("y", lambda E: E.dma_start(out=IDF[:, :], in_=idf_d))
        t_ones = dma("y", lambda E: E.dma_start(out=ONES[:, :], in_=onesq_d))
        t_blk8 = dma("y", lambda E: E.dma_start(out=BLK8[:, :], in_=blk8_d))
        t_cos = dma("y", lambda E: E.dma_start(
            out=COSH[:, :].rearrange("p (c n) -> p c n", c=NT),
            in_=cosh_d.rearrange("(c p) n -> p c n", p=128)))
        t_sinn = dma("y", lambda E: E.dma_start(
            out=SINN[:, :].rearrange("p (c n) -> p c n", c=NT),
            in_=sinn_d.rearrange("(c p) n -> p c n", p=128)))
        t_sinp = dma("y", lambda E: E.dma_start(
            out=SINP[:, :].rearrange("p (c n) -> p c n", c=NT),
            in_=sinp_d.rearrange("(c p) n -> p c n", p=128)))
        t_gate = dma("g", lambda E: E.dma_start(
            out=GATE[:, :].rearrange("p (c n) -> p c n", c=NH),
            in_=gate_d.rearrange("(c p) n -> p c n", p=128)))
        t_wr = dma("g", lambda E: E.dma_start(
            out=WRS[:, :].rearrange("p (c n) -> p c n", c=NH),
            in_=wr_d.rearrange("(c p) n -> p c n", p=128)))
        t_brr = dma("g", lambda E: E.dma_start(out=BRR[:, :], in_=br_d))
        t_x = dma("y", lambda E: E.dma_start(
            out=X[:, :].rearrange("p (c n) -> p c n", c=NT),
            in_=x_d.rearrange("(c p) n -> p c n", p=128)))

        x_wtok = [t_x] * NT

        def emit_ln():
            last = None
            prev_rs = None
            for tc in range(NT):
                Xc = X[:, tc * H:(tc + 1) * H]
                W("v", x_wtok[tc])
                W("v", prev_rs)
                vop(lambda E, Xc=Xc: E.reduce_sum(ST1[:, 0:1], Xc, axis=AX))
                t_sq = vop(lambda E, Xc=Xc: E.tensor_mul(XLN[:, :], Xc, Xc))
                W("v", t_sq)
                t_s1 = vop(lambda E: E.reduce_sum(ST1[:, 1:2], XLN[:, :],
                                                  axis=AX))
                W("v", t_s1)
                vop(lambda E: E.tensor_scalar(ST1[:, 2:3], ST1[:, 0:1],
                                              1.0 / H, None, ALU.mult))
                vop(lambda E: E.tensor_scalar(ST1[:, 3:4], ST1[:, 1:2],
                                              1.0 / H, None, ALU.mult))
                vop(lambda E: E.tensor_mul(ST1[:, 4:5], ST1[:, 2:3],
                                           ST1[:, 2:3]))
                vop(lambda E: E.tensor_sub(ST1[:, 5:6], ST1[:, 3:4],
                                           ST1[:, 4:5]))
                t_vp = vop(lambda E: E.tensor_scalar(ST1[:, 5:6], ST1[:, 5:6],
                                                     1e-5, None, ALU.add))
                W("s", t_vp)
                t_l = sop(lambda E: E.activation(ST1[:, 6:7], ST1[:, 5:6],
                                                 AF.Ln))
                W("s", t_l)
                t_rs = sop(lambda E: E.activation(ST1[:, 6:7], ST1[:, 6:7],
                                                  AF.Exp, scale=-0.5))
                prev_rs = t_rs
                W("v", t_rs)
                t_nm = vop(lambda E: E.scalar_tensor_tensor(
                    ST1[:, 7:8], ST1[:, 2:3], -1.0, ST1[:, 6:7],
                    ALU.mult, ALU.mult))
                W("v", t_nm)
                t_ap = vop(lambda E, Xc=Xc: E.scalar_tensor_tensor(
                    XLN[:, :], Xc, ST1[:, 6:7],
                    ST1[:, 7:8].broadcast_to([128, H]), ALU.mult, ALU.add))
                for half in range(2):
                    W("t", t_ap)
                    W("t", t_iden)
                    tts = None
                    for k in range(8):
                        hc = half * 8 + k
                        tts = top(lambda E, k=k, hc=hc: E.transpose(
                            PTB[:, k * 128:(k + 1) * 128],
                            XLN[:, hc * 128:(hc + 1) * 128], IDEN[:, :]),
                            inc=(k == 7))
                    W("v", tts)
                    t_cp = vop(lambda E, tc=tc, half=half: E.tensor_copy(
                        XT[:, :].rearrange("p (c n) -> p c n", c=NH)[
                            :, half * 8:half * 8 + 8, tc * 128:(tc + 1) * 128],
                        PTB[:, :].rearrange("p (k n) -> p k n", k=8)))
                    W("t", t_cp)
                    last = t_cp
            return last

        GCTR = [0]
        W2CTR = [0]
        A_RT = {}
        B_BT = {}

        def emit_ffn(xt_tok, wguard=None, dbg=False):
            W("t", t_gate)
            W("t", xt_tok)
            lgv = {}
            t11 = None
            for tc in range(NT):
                bank = P2[:, (tc % 2) * 512:(tc % 2) * 512 + 32]
                W("t", lgv.get(tc % 2))
                for hc in range(NH):
                    top(lambda E, b=bank, hc=hc, tc=tc: E.matmul(
                        b, XT[:, hc * TLOC + tc * 128:hc * TLOC + (tc + 1) * 128],
                        GATE[:, hc * 32:(hc + 1) * 32],
                        start=(hc == 0), stop=(hc == NH - 1)),
                        inc=(hc == NH - 1))
                t_mm = tcur()
                W("v", t_mm)
                t_cp = vop(lambda E, b=bank, tc=tc: E.tensor_copy(
                    LOGIT[:, tc * 32:(tc + 1) * 32], b))
                lgv[tc % 2] = t_cp
                Lc = LOGIT[:, tc * 32:(tc + 1) * 32]
                W("v", t_cp)
                vop(lambda E, Lc=Lc: E.reduce_max(GS[:, 0:1], Lc, axis=AX))
                t1 = vop(lambda E, Lc=Lc: E.tensor_scalar(
                    GT1[:, :], Lc, GS[:, 0:1], None, ALU.subtract))
                W("s", t1)
                t2 = sop(lambda E: E.activation(GT2[:, :], GT1[:, :], AF.Exp))
                W("v", t2)
                vop(lambda E: E.reduce_sum(GS[:, 1:2], GT2[:, :], axis=AX))
                vop(lambda E: E.reciprocal(GS[:, 2:3], GS[:, 1:2]))
                t3 = vop(lambda E: E.tensor_scalar(
                    GSC[:, :], GT2[:, :], GS[:, 2:3], None, ALU.mult))
                W("v", t3)
                Sv = GSC[:, :].rearrange("p (a b) -> p a b", b=2)
                vop(lambda E, Sv=Sv: E.tensor_max(
                    GG[:, :].rearrange("p (a b) -> p a b", b=1),
                    Sv[:, :, 0:1], Sv[:, :, 1:2]))
                Gv = GG[:, :].rearrange("p (a b) -> p a b", b=2)
                t4 = vop(lambda E, Gv=Gv: E.tensor_max(
                    GH[:, :].rearrange("p (a b) -> p a b", b=1),
                    Gv[:, :, 0:1], Gv[:, :, 1:2]))
                W("v", t4)
                vop(lambda E: E.reduce_max(GS[:, 3:4], GH[:, :], axis=AX))
                vop(lambda E: E.tensor_scalar(
                    GH2[:, :], GH[:, :], GS[:, 3:4], None, ALU.is_ge))
                vop(lambda E: E.scalar_tensor_tensor(
                    GH2[:, :], GH2[:, :], -1e30, GH[:, :], ALU.mult, ALU.add))
                vop(lambda E: E.reduce_max(GS[:, 3:4], GH2[:, :], axis=AX))
                vop(lambda E: E.tensor_scalar(GT3[:, 0:8], GH2[:, :],
                                              GS[:, 3:4], None, ALU.is_ge))
                vop(lambda E: E.scalar_tensor_tensor(
                    GH2[:, :], GT3[:, 0:8], -1e30, GH2[:, :],
                    ALU.mult, ALU.add))
                t5 = vop(lambda E: E.reduce_max(GS[:, 3:4], GH2[:, :], axis=AX))
                W("v", t5)
                t6 = vop(lambda E: E.tensor_scalar(
                    GH[:, :], GH[:, :], GS[:, 3:4], None, ALU.is_ge))
                W("v", t6)
                for r in range(4):
                    vop(lambda E, r=r: E.tensor_copy(
                        GT1[:, :].rearrange("p (a b) -> p a b", b=4)[
                            :, :, r:r + 1],
                        GH[:, :].rearrange("p (a b) -> p a b", b=1)))
                t7 = vop(lambda E: E.tensor_mul(GT2[:, :], GSC[:, :], GT1[:, :]))
                W("v", t7)
                t8 = vop(lambda E: E.tensor_copy(GT1[:, :], GT2[:, :]))
                W("v", t8)
                for it in range(5):
                    vop(lambda E: E.reduce_max(GS[:, 4:5], GT1[:, :], axis=AX))
                    vop(lambda E: E.tensor_scalar(
                        GT3[:, :], GT1[:, :], GS[:, 4:5], None, ALU.is_ge))
                    t9 = vop(lambda E: E.scalar_tensor_tensor(
                        GT1[:, :], GT3[:, :], -1e30, GT1[:, :],
                        ALU.mult, ALU.add))
                    W("v", t9)
                vop(lambda E: E.reduce_max(GS[:, 4:5], GT1[:, :], axis=AX))
                vop(lambda E: E.tensor_scalar(
                    GT3[:, :], GT2[:, :], GS[:, 4:5], None, ALU.is_ge))
                t10 = vop(lambda E: E.tensor_mul(GT1[:, :], GT2[:, :],
                                                 GT3[:, :]))
                W("v", t10)
                vop(lambda E: E.reduce_sum(GS[:, 5:6], GT1[:, :], axis=AX))
                vop(lambda E: E.tensor_scalar(GS[:, 5:6], GS[:, 5:6], 1e-20,
                                              None, ALU.add))
                vop(lambda E: E.reciprocal(GS[:, 6:7], GS[:, 5:6]))
                t11 = vop(lambda E, tc=tc: E.tensor_scalar(
                    COMB[:, tc * 32:(tc + 1) * 32], GT1[:, :], GS[:, 6:7],
                    0.5, ALU.mult, ALU.mult))
            comb_tok = t11
            if DEBUG and dbg:
                W("y", comb_tok)
                dma("y", lambda E: E.dma_start(out=d_comb, in_=COMB[:, :]))

            a_ev = {}
            b_ev = {}
            ht_rd = [None, None]
            last_a_ev = None
            last_b = None
            for e in range(NEALL):
                par = e % 2
                for ic in range(8):
                    gi = GCTR[0]
                    GCTR[0] += 1
                    buf = W13[gi % 3]
                    if wguard is not None:
                        W("g", wguard)
                    if gi >= 3:
                        W("g", A_RT[gi - 3])
                    tw1 = dma("g", lambda E, e=e, ic=ic, buf=buf: E.dma_start(
                        out=buf[:, 0:2048].rearrange("p (c n) -> p c n", c=NH),
                        in_=w1_d[e][:, ic * 128:(ic + 1) * 128].rearrange(
                            "(c p) n -> p c n", p=128)))
                    tw3 = dma("g", lambda E, e=e, ic=ic, buf=buf: E.dma_start(
                        out=buf[:, 2048:4096].rearrange("p (c n) -> p c n",
                                                        c=NH),
                        in_=w3_d[e][:, ic * 128:(ic + 1) * 128].rearrange(
                            "(c p) n -> p c n", p=128)))
                    for th in range(2):
                        pb0 = P0[:, th * 512:(th + 1) * 512]
                        pb1 = P1[:, th * 512:(th + 1) * 512]
                        W("t", a_ev.get(th))
                        W("t", tw1)
                        W("t", tw3)
                        W("t", xt_tok)
                        for hc in range(NH):
                            top(lambda E, pb0=pb0, hc=hc, th=th, buf=buf:
                                E.matmul(
                                    pb0, buf[:, hc * 128:(hc + 1) * 128],
                                    XT[:, hc * TLOC + th * 512:
                                       hc * TLOC + (th + 1) * 512],
                                    start=(hc == 0), stop=(hc == NH - 1)))
                            top(lambda E, pb1=pb1, hc=hc, th=th, buf=buf:
                                E.matmul(
                                    pb1,
                                    buf[:, 2048 + hc * 128:
                                        2048 + (hc + 1) * 128],
                                    XT[:, hc * TLOC + th * 512:
                                       hc * TLOC + (th + 1) * 512],
                                    start=(hc == 0), stop=(hc == NH - 1)),
                                inc=(hc == NH - 1))
                        t_mm = tcur()
                        A_RT[gi] = t_mm
                        W("s", t_mm)
                        W("s", ht_rd[par])
                        W("s", last_a_ev)
                        t_si = sop(lambda E, pb0=pb0: E.activation(
                            SILF[:, :], pb0, AF.Silu))
                        W("v", t_si)
                        W("v", ht_rd[par])
                        t_ev = vop(lambda E, pb1=pb1, par=par, ic=ic, th=th:
                                   E.tensor_mul(
                                       HT[par][:, ic * 1024 + th * 512:
                                               ic * 1024 + (th + 1) * 512],
                                       SILF[:, :], pb1))
                        a_ev[th] = t_ev
                        last_a_ev = t_ev
                        if DEBUG and dbg and e == 0 and ic == 7 and th == 1:
                            W("g", t_ev)
                            dma("g", lambda E: E.dma_start(out=d_ht,
                                                           in_=HT[0][:, :]))
                for hb in range(4):
                    wi = W2CTR[0]
                    W2CTR[0] += 1
                    wbuf = W2B[wi % 2]
                    if wguard is not None:
                        W("g", wguard)
                    if wi >= 2:
                        W("g", B_BT[wi - 2])
                    tw2 = dma("g", lambda E, e=e, hb=hb, wbuf=wbuf: E.dma_start(
                        out=wbuf[:, :].rearrange("p (c n) -> p c n", c=8),
                        in_=w2_d[e][:, hb * 512:(hb + 1) * 512].rearrange(
                            "(c p) n -> p c n", p=128)))
                    for tc in range(NT):
                        bank = P2[:, (tc % 2) * 512:(tc % 2 + 1) * 512]
                        W("t", b_ev.get(tc % 2))
                        if e == 0 and hb == 0:
                            W("t", comb_tok)
                        W("t", tw2)
                        W("t", last_a_ev)
                        for ic in range(8):
                            top(lambda E, bank=bank, ic=ic, tc=tc, par=par,
                                wbuf=wbuf: E.matmul(
                                    bank,
                                    HT[par][:, ic * 1024 + tc * 128:
                                            ic * 1024 + (tc + 1) * 128],
                                    wbuf[:, ic * 512:(ic + 1) * 512],
                                    start=(ic == 0), stop=(ic == 7)),
                                inc=(ic == 7))
                        t_mm = tcur()
                        last_b = t_mm
                        B_BT[wi] = t_mm
                        W("v", t_mm)
                        W("v", x_wtok[tc])
                        sl = (tc * H + hb * 512, tc * H + (hb + 1) * 512)
                        if e < NE:
                            t_ev = vop(lambda E, bank=bank, tc=tc, e=e, sl=sl:
                                       E.scalar_tensor_tensor(
                                           X[:, sl[0]:sl[1]], bank,
                                           COMB[:, tc * 32 + e:tc * 32 + e + 1],
                                           X[:, sl[0]:sl[1]],
                                           ALU.mult, ALU.add))
                        else:
                            t_ev = vop(lambda E, bank=bank, sl=sl:
                                       E.scalar_tensor_tensor(
                                           X[:, sl[0]:sl[1]], bank, 0.5,
                                           X[:, sl[0]:sl[1]],
                                           ALU.mult, ALU.add))
                        b_ev[tc % 2] = t_ev
                        x_wtok[tc] = t_ev
                ht_rd[par] = last_b
            return tcur()

        def emit_attn(xt_tok, guard):
            ws_rd = [None, None]
            wsctr = [0]
            misc = {}

            def load_ws(wd, bias_d, nb, width):
                i = wsctr[0] % 2
                wsctr[0] += 1
                W("g", guard)
                W("g", ws_rd[i])
                t = dma("g", lambda E, wd=wd, nb=nb, width=width, i=i:
                        E.dma_start(
                            out=WS[i][:, 0:NH * width].rearrange(
                                "p (c n) -> p c n", c=NH),
                            in_=wd[:, nb * width:(nb + 1) * width].rearrange(
                                "(c p) n -> p c n", p=128)))
                tb = dma("g", lambda E, bias_d=bias_d, nb=nb, width=width, i=i:
                         E.dma_start(out=BR1[0:1, i * 256:i * 256 + width],
                                     in_=bias_d[:, nb * width:(nb + 1) * width]))
                return i, t, tb

            kn_init = vop(lambda E: E.memset(KN2L[:, :], 0.0))
            pb_cycle = [P0[:, 0:256], P0[:, 512:768], P1[:, 0:256],
                        P1[:, 512:768]]
            pb_ev = [None] * 4
            ccin_toks = []
            vstg_rd = [None]
            kstg_rd = [None]

            def proj_block(wd, bias_d, nb, width, kind):
                i, tw, tb = load_ws(wd, bias_d, nb, width)
                last_mm = None
                for tc in range(NT):
                    pbank = pb_cycle[tc % 4][:, 0:width]
                    W("t", pb_ev[tc % 4])
                    W("t", tw)
                    W("t", tb)
                    W("t", xt_tok)
                    for hc in range(NH):
                        top(lambda E, pbank=pbank, hc=hc, tc=tc, i=i,
                            width=width: E.matmul(
                                pbank,
                                XT[:, hc * TLOC + tc * 128:
                                   hc * TLOC + (tc + 1) * 128],
                                WS[i][:, hc * width:(hc + 1) * width],
                                start=(hc == 0), stop=False))
                    top(lambda E, pbank=pbank, i=i, width=width: E.matmul(
                        pbank, ONES[0:1, 0:128], BR1[0:1, i * 256:i * 256 + width],
                        start=False, stop=True), inc=True)
                    t_mm = tcur()
                    last_mm = t_mm
                    if kind == "v":
                        W("s", t_mm)
                        W("s", vstg_rd[0])
                        t_cv = sop(lambda E, pbank=pbank, width=width: E.copy(
                            VSTG[:, 0:width], pbank))
                        pb_ev[tc % 4] = t_cv
                        W("y", t_cv)
                        W("y", guard)
                        t_d = dma("y", lambda E, tc=tc, nb=nb, width=width:
                                  E.dma_start(
                                      out=ccin.ap()[
                                          1024 + tc * 128:1024 + (tc + 1) * 128,
                                          nb * width:(nb + 1) * width],
                                      in_=VSTG[:, 0:width]))
                        vstg_rd[0] = t_d
                        ccin_toks.append(t_d)
                        continue
                    W("s", t_mm)
                    W("s", misc.get("last_rp"))
                    t_cp = sop(lambda E, pbank=pbank, width=width: E.copy(
                        SILF[:, 0:width], pbank))
                    pb_ev[tc % 4] = t_cp
                    W("v", t_cp)
                    W("v", t_cos)
                    W("v", t_sinn)
                    W("v", t_sinp)
                    hh = width // 128
                    for j in range(hh):
                        o = j * 128
                        vop(lambda E, o=o, tc=tc: E.tensor_mul(
                            ROPA[:, o:o + 128].rearrange(
                                "p (a n) -> p a n", n=64),
                            SILF[:, o:o + 128].rearrange(
                                "p (a n) -> p a n", n=64),
                            COSH[:, tc * 64:(tc + 1) * 64].rearrange(
                                "p (a n) -> p a n", a=1).broadcast_to(
                                    [128, 2, 64])))
                        vop(lambda E, o=o, tc=tc: E.tensor_mul(
                            ROPB[:, o:o + 64], SILF[:, o + 64:o + 128],
                            SINN[:, tc * 64:(tc + 1) * 64]))
                        vop(lambda E, o=o, tc=tc: E.tensor_mul(
                            ROPB[:, o + 64:o + 128], SILF[:, o:o + 64],
                            SINP[:, tc * 64:(tc + 1) * 64]))
                    t_rp = vop(lambda E, width=width: E.tensor_add(
                        XLN[:, 0:width], ROPA[:, 0:width], ROPB[:, 0:width]))
                    misc["last_rp"] = t_rp
                    if kind == "k":
                        W("v", t_rp)
                        W("v", kn_init)
                        t_q2 = vop(lambda E, width=width: E.tensor_mul(
                            KSQ2[:, 0:width], XLN[:, 0:width], XLN[:, 0:width]))
                        W("v", t_q2)
                        for j in range(hh):
                            vop(lambda E, j=j: E.reduce_sum(
                                KN2S[:, j:j + 1],
                                KSQ2[:, j * 128:(j + 1) * 128], axis=AX))
                        vop(lambda E, nb=nb, hh=hh: E.tensor_max(
                            KN2L[:, nb * hh:nb * hh + hh],
                            KN2L[:, nb * hh:nb * hh + hh], KN2S[:, 0:hh]))
                    W("t", t_rp)
                    tts = None
                    for j in range(hh):
                        tts = top(lambda E, j=j: E.transpose(
                            PTB[:, j * 128:(j + 1) * 128],
                            XLN[:, j * 128:(j + 1) * 128], IDEN[:, :]),
                            inc=(j == hh - 1))
                    W("v", tts)
                    if kind == "q":
                        t_c2 = vop(lambda E, nb=nb, tc=tc, hh=hh: E.tensor_copy(
                            QT[:, :].rearrange("p (h n) -> p h n", h=HEADS)[
                                :, nb * hh:nb * hh + hh,
                                tc * 128:(tc + 1) * 128],
                            PTB[:, 0:hh * 128].rearrange(
                                "p (k n) -> p k n", k=hh)))
                    else:
                        W("v", kstg_rd[0])
                        t_c2 = vop(lambda E, tc=tc, hh=hh: E.tensor_copy(
                            KSTG[:, :].rearrange("p (h n) -> p h n", h=2)[
                                :, 0:hh, tc * 128:(tc + 1) * 128],
                            PTB[:, 0:hh * 128].rearrange(
                                "p (k n) -> p k n", k=hh)))
                    W("t", t_c2)
                    misc["last_c2"] = t_c2
                if kind == "k":
                    for j in range(2):
                        head = nb * 2 + j
                        W("y", misc["last_c2"])
                        W("y", guard)
                        t_d = dma("y", lambda E, head=head, j=j: E.dma_start(
                            out=ccin.ap()[head * 128:(head + 1) * 128, :],
                            in_=KSTG[:, j * 1024:(j + 1) * 1024]))
                        kstg_rd[0] = t_d
                        ccin_toks.append(t_d)
                ws_rd[i] = last_mm
                return last_mm

            for nb in range(4):
                proj_block(wk_d, bk_d, nb, 256, "k")
            for nb in range(4):
                proj_block(wv_d, bv_d, nb, 256, "v")

            W("t", t_wr)
            W("t", t_brr)
            r_last = None
            rs_rd = None
            for tc in range(NT):
                W("t", r_last)
                for hc in range(NH):
                    top(lambda E, hc=hc, tc=tc: E.matmul(
                        P3[:, 0:16],
                        XT[:, hc * TLOC + tc * 128:hc * TLOC + (tc + 1) * 128],
                        WRS[:, hc * 16:(hc + 1) * 16],
                        start=(hc == 0), stop=False))
                top(lambda E: E.matmul(
                    P3[:, 0:16], ONES[0:1, 0:128], BRR[0:1, 0:16],
                    start=False, stop=True), inc=True)
                t_mm = tcur()
                W("v", t_mm)
                r_last = vop(lambda E, tc=tc: E.tensor_copy(
                    RLOG[:, tc * 16:(tc + 1) * 16], P3[:, 0:16]))
            rt_last = None
            for tc in range(NT):
                W("t", r_last)
                W("t", rt_last)
                W("t", t_idf)
                t_tr = top(lambda E, tc=tc: E.transpose(
                    P2[0:16, 0:128], RLOG[:, tc * 16:(tc + 1) * 16],
                    IDF[:, :]), inc=True)
                W("v", t_tr)
                W("v", rs_rd)
                t_cpr = vop(lambda E: E.tensor_copy(RSM[:, :],
                                                    P2[0:16, 0:128]))
                rt_last = t_cpr
                W("y", t_cpr)
                t_d = dma("y", lambda E, tc=tc: E.dma_start(
                    out=ccin.ap()[2048:2080, :].bitcast(F32).rearrange(
                        "(r two) n -> r (two n)", two=2)[
                            :, tc * 128:(tc + 1) * 128],
                    in_=RSM[:, :]))
                rs_rd = t_d
                ccin_toks.append(t_d)
            W("t", rt_last)
            t_kt = top(lambda E: E.transpose(P2[0:8, 0:128], KN2L[:, :],
                                             IDF[:, :]), inc=True)
            W("v", t_kt)
            W("v", rs_rd)
            t_kc = vop(lambda E: E.tensor_copy(RSM[0:8, 0:128],
                                               P2[0:8, 0:128]))
            W("v", t_kc)
            t_kr = vop(lambda E: E.reduce_max(KN2S[0:8, 0:1],
                                              RSM[0:8, 0:128], axis=AX))
            W("s", t_kr)
            t_kra = dma("s", lambda E: E.dma_start(
                out=krowd.ap(), in_=KN2S[0:8, 0:1]))
            W("s", t_kra)
            t_krd = dma("s", lambda E: E.dma_start(
                out=KROW[0:1, 0:8],
                in_=krowd.ap().rearrange("(o p) n -> o (p n)", o=1)))
            W("y", t_krd)
            t_d = dma("y", lambda E: E.dma_start(
                out=ccin.ap()[2080:2081, :].bitcast(F32)[:, 0:8],
                in_=KROW[0:1, 0:8]))
            ccin_toks.append(t_d)

            for t in ccin_toks:
                W("g", t)
            cc_tok = op("g", lambda E: E.collective_compute(
                "AllGather", ALU.bypass,
                replica_groups=[[0, 1, 2, 3], [4, 5, 6, 7]],
                ins=[ccin.ap()], outs=[ccout.ap()]), "scc", 1)

            for nb in range(8):
                proj_block(wq_d, bq_d, nb, 256, "q")
            qt_tok = misc["last_c2"]
            for i in range(4):
                W("t", pb_ev[i])

            W("y", cc_tok)
            t_rl = None
            for r in range(4):
                t_rl = dma("y", lambda E, r=r: E.dma_start(
                    out=RL[r * 32:(r + 1) * 32, :],
                    in_=ccout.ap().bitcast(F32)[
                        r * CCR + 2048:r * CCR + 2080, :]))
            t_kg = None
            for r in range(4):
                t_kg = dma("y", lambda E, r=r: E.dma_start(
                    out=KN2G[0:1, :].rearrange(
                        "o (k r) -> o k r", r=4)[:, :, r:r + 1],
                    in_=ccout.ap().bitcast(F32)[
                        r * CCR + 2080:r * CCR + 2081, 0:8],
                    allow_slow_non_contiguous=True))
            W("v", t_kg)
            t_kx = vop(lambda E: E.reduce_max(
                KMAX2[0:1, :],
                KN2G[0:1, :].rearrange("o (a b) -> o a b", b=4), axis=AX))
            W("v", t_kx)
            vop(lambda E: E.tensor_scalar(CC12[0:1, 0:8], KMAX2[0:1, :],
                                          1.0 / DH, None, ALU.mult))
            t_c = vop(lambda E: E.tensor_scalar(CC12[0:1, 8:16], KMAX2[0:1, :],
                                                float(DH), None, ALU.mult))
            W("s", t_c)
            t_sq = sop(lambda E: E.activation(CC12[0:1, 16:32],
                                              CC12[0:1, 0:16], AF.Sqrt))
            W("v", t_sq)
            t_c12 = vop(lambda E: E.tensor_scalar(
                CC12[0:1, 0:16], CC12[0:1, 16:32], -0.3, None, ALU.mult))

            W("v", t_rl)
            vop(lambda E: E.memset(RTHR[:, 0:1], 0.0))
            t_bi = vop(lambda E: E.memset(RTHR[:, 1:2], 0.0))
            cnt_tok = t_bi
            cm_rd = None
            for it in range(33):
                W("v", cnt_tok)
                W("v", cm_rd)
                vop(lambda E: E.tensor_scalar(RCMP[:, :], RL[:, :],
                                              RTHR[:, 0:1], None, ALU.is_ge))
                t_r = vop(lambda E: E.reduce_sum(KN2S[:, 2:3], RCMP[:, :],
                                                 axis=AX))
                W("t", t_r)
                W("t", t_blk8)
                W("t", cm_rd)
                top(lambda E: E.matmul(P3[:, 0:1], BLK8[:, :], KN2S[:, 2:3],
                                       start=True, stop=True), inc=True)
                t_cm = tcur()
                W("v", t_cm)
                t_dir = vop(lambda E: E.tensor_scalar(
                    KN2S[:, 3:4], P3[:, 0:1], float(TOPK), None, ALU.is_gt))
                cm_rd = t_dir
                delta = 64.0 * (0.5 ** it)
                vop(lambda E, delta=delta: E.tensor_scalar(
                    KN2S[:, 4:5], KN2S[:, 3:4], 2.0 * delta, delta,
                    ALU.mult, ALU.subtract))
                cnt_tok = vop(lambda E: E.tensor_add(
                    RTHR[:, 0:1], RTHR[:, 0:1], KN2S[:, 4:5]))
            W("v", cnt_tok)
            if DEBUG:
                W("y", cnt_tok)
                dma("y", lambda E: E.dma_start(out=d_rl, in_=RL[:, :]))
                dma("y", lambda E: E.dma_start(out=d_thr, in_=RTHR[:, :]))
                W("y", t_c12)
                dma("y", lambda E: E.dma_start(out=d_kx, in_=KN2G[0:1, :]))
            t_mk = vop(lambda E: E.tensor_scalar(
                MASKB[:, :], RL[:, :], RTHR[:, 0:1], -1e9,
                ALU.is_lt, ALU.mult))
            W("y", t_mk)
            t_md = dma("y", lambda E: E.dma_start(out=masknd.ap()[:, :],
                                                  in_=MASKB[:, :]))
            t_on = dma("s", lambda E: E.dma_start(out=RHS2[0:1, :],
                                                  in_=onesrow_d[:, 0:TLOC]))

            ktv_rd = None
            kv_tok = None
            st_ev = [None, None]
            out_ev = [None, None]
            den_ev = cm_rd
            pte_rd = [None, None]
            b2_rd = None
            nm_tok = None
            att_toks = []
            t_av = None
            for head in range(HEADS):
                kv = head // 2
                if head % 2 == 0:
                    W("y", ktv_rd)
                    W("y", cc_tok)
                    W("y", guard)
                    for r in range(4):
                        dma("y", lambda E, r=r, kv=kv: E.dma_start(
                            out=KTKV[:, r * 1024:(r + 1) * 1024],
                            in_=ccout.ap()[r * CCR + kv * 128:
                                           r * CCR + (kv + 1) * 128, :]))
                        kv_tok = dma("y", lambda E, r=r, kv=kv: E.dma_start(
                            out=VKV[:, r * 1024:(r + 1) * 1024].rearrange(
                                "p (lc d) -> p lc d", d=128),
                            in_=ccout.ap()[r * CCR + 1024:r * CCR + 2048,
                                           kv * 128:(kv + 1) * 128].rearrange(
                                               "(lc p) d -> p lc d", p=128)))
                W("s", b2_rd)
                W("s", t_md)
                b2a = dma("s", lambda E, head=head: E.dma_start(
                    out=BIAS2[0:1, :].rearrange(
                        "o (r g n) -> o r g n", r=4, g=2),
                    in_=masknd.ap().rearrange(
                        "(r g) n -> r g n", g=32)[
                            :, 2 * head:2 * head + 2, :]))
                b2b = dma("s", lambda E: E.dma_start(
                    out=BIAS2[1:2, :], in_=onesrow_d[:, :]))
                W("v", qt_tok)
                W("v", nm_tok)
                t_qs = vop(lambda E, head=head: E.tensor_mul(
                    QSQ[:, :], QT[:, head * 1024:(head + 1) * 1024],
                    QT[:, head * 1024:(head + 1) * 1024]))
                W("t", t_qs)
                W("t", den_ev)
                t_qn = None
                for qh in range(2):
                    t_qn = top(lambda E, qh=qh: E.matmul(
                        P1[0:1, qh * 512:(qh + 1) * 512], ONES[:, 0:1],
                        QSQ[:, qh * 512:(qh + 1) * 512],
                        start=True, stop=True), inc=(qh == 1))
                W("v", t_qn)
                W("v", t_c12)
                nm_tok = vop(lambda E, kv=kv: E.scalar_tensor_tensor(
                    NMB[0:1, :], P1[0:1, 0:1024], CC12[0:1, kv:kv + 1],
                    CC12[0:1, 8 + kv:9 + kv].broadcast_to([1, TLOC]),
                    ALU.mult, ALU.add))
                W("s", nm_tok)
                t_r2 = dma("s", lambda E: E.dma_start(out=RHS2[1:2, :],
                                                      in_=NMB[0:1, :]))
                for qh in range(2):
                    op2 = P2[:, qh * 512:(qh + 1) * 512]
                    for c in range(32):
                        pp = c % 2
                        stb = P0[:, pp * 512:(pp + 1) * 512]
                        W("t", st_ev[pp])
                        W("t", kv_tok)
                        W("t", b2a)
                        W("t", b2b)
                        W("t", t_r2)
                        W("t", t_on)
                        if c == 0:
                            W("t", out_ev[qh])
                            W("t", den_ev)
                        top(lambda E, stb=stb, c=c, head=head, qh=qh:
                            E.matmul(
                                stb, KTKV[:, c * 128:(c + 1) * 128],
                                QT[:, head * 1024 + qh * 512:
                                   head * 1024 + (qh + 1) * 512],
                                start=True, stop=False))
                        t_st = top(lambda E, stb=stb, c=c, qh=qh: E.matmul(
                            stb, BIAS2[:, c * 128:(c + 1) * 128],
                            RHS2[:, qh * 512:(qh + 1) * 512],
                            start=False, stop=True), inc=True)
                        W("s", t_st)
                        W("s", pte_rd[pp])
                        t_ex = sop(lambda E, stb=stb, pp=pp: E.activation(
                            PTE[:, pp * 512:(pp + 1) * 512], stb, AF.Exp,
                            scale=INV_SQRT_D))
                        st_ev[pp] = t_ex
                        W("t", t_ex)
                        top(lambda E, op2=op2, c=c, pp=pp: E.matmul(
                            op2, VKV[:, c * 128:(c + 1) * 128],
                            PTE[:, pp * 512:(pp + 1) * 512],
                            start=(c == 0), stop=(c == 31)))
                        t_av = top(lambda E, c=c, pp=pp: E.matmul(
                            P3[:, :], ONES[:, :],
                            PTE[:, pp * 512:(pp + 1) * 512],
                            start=(c == 0), stop=(c == 31)), inc=True)
                        pte_rd[pp] = t_av
                    W("v", t_av)
                    t_rc = vop(lambda E: E.reciprocal(RECIP[:, :], P3[:, :]))
                    den_ev = t_rc
                    W("v", t_rc)
                    t_no = vop(lambda E, op2=op2, head=head, qh=qh:
                               E.tensor_mul(
                                   XT[:, head * 1024 + qh * 512:
                                      head * 1024 + (qh + 1) * 512],
                                   op2, RECIP[:, :]))
                    out_ev[qh] = t_no
                    att_toks.append(t_no)
                ktv_rd = t_av
                b2_rd = t_av
            at_last = att_toks[-1]
            if DEBUG:
                W("g", at_last)
                W("g", qt_tok)
                dma("g", lambda E: E.dma_start(out=d_att, in_=XT[:, :]))
                dma("g", lambda E: E.dma_start(out=d_qt, in_=QT[:, :]))
                dma("g", lambda E: E.dma_start(out=d_kt, in_=KTKV[:, :]))

            head_end = ("st", P.cnt["st"])
            W("t", st_ev[0])
            W("t", st_ev[1])
            o_ev = [None, None]
            o_last = None
            for nb in range(8):
                i, tw, tb = load_ws(wo_d, bo_d, nb, 256)
                for tc in range(NT):
                    pb = [P0[:, 0:256], P0[:, 512:768]][tc % 2]
                    W("t", o_ev[tc % 2])
                    W("t", tw)
                    W("t", tb)
                    W("t", at_last)
                    for hc in range(NH):
                        top(lambda E, pb=pb, hc=hc, tc=tc, i=i: E.matmul(
                            pb, XT[:, hc * TLOC + tc * 128:
                                   hc * TLOC + (tc + 1) * 128],
                            WS[i][:, hc * 256:(hc + 1) * 256],
                            start=(hc == 0), stop=False))
                    top(lambda E, pb=pb, i=i: E.matmul(
                        pb, ONES[0:1, 0:128], BR1[0:1, i * 256:i * 256 + 256],
                        start=False, stop=True), inc=True)
                    t_mm = tcur()
                    o_last = t_mm
                    W("v", t_mm)
                    W("v", x_wtok[tc])
                    t_ev = vop(lambda E, pb=pb, tc=tc, nb=nb: E.tensor_add(
                        X[:, tc * H + nb * 256:tc * H + (nb + 1) * 256],
                        X[:, tc * H + nb * 256:tc * H + (nb + 1) * 256], pb))
                    o_ev[tc % 2] = t_ev
                    x_wtok[tc] = t_ev
                ws_rd[i] = o_last
            return o_last, head_end

        xt1 = emit_ln()
        if DEBUG:
            W("g", xt1)
            dma("g", lambda E: E.dma_start(out=d_xt, in_=XT[:, :]))
        ffn1_end = emit_ffn(xt1, dbg=True)
        if DEBUG:
            for tc in range(NT):
                W("y", x_wtok[tc])
            dma("y", lambda E: E.dma_start(
                out=d_x1.rearrange("(c p) n -> p c n", p=128),
                in_=X[:, :].rearrange("p (c n) -> p c n", c=NT)))
        xt2 = emit_ln()
        o_last, head_end = emit_attn(xt2, ffn1_end)
        if DEBUG:
            for tc in range(NT):
                W("y", x_wtok[tc])
            dma("y", lambda E: E.dma_start(
                out=d_x2.rearrange("(c p) n -> p c n", p=128),
                in_=X[:, :].rearrange("p (c n) -> p c n", c=NT)))
        xt3 = emit_ln()
        emit_ffn(xt3, wguard=head_end)
        for tc in range(NT):
            W("y", x_wtok[tc])
        t_out = dma("y", lambda E: E.dma_start(
            out=out_d.rearrange("(c p) n -> p c n", p=128),
            in_=X[:, :].rearrange("p (c n) -> p c n", c=NT)))
        W("y", t_out)

        def runner(key):
            def run(E):
                for item in P.q[key]:
                    if item[0] == "WAIT":
                        E.wait_ge(sems[item[1]], item[2])
                    else:
                        fn, semname, amt = item
                        ins = fn(E)
                        if semname:
                            ins.then_inc(sems[semname], amt)
            return run

        block.tensor(runner("t"))
        block.vector(runner("v"))
        block.scalar(runner("s"))
        block.gpsimd(runner("g"))
        block.sync(runner("y"))

    return nc


_CACHE = {}


def kernel(**inputs):
    x = np.asarray(inputs["L0_attn_hidden_states"])
    w1 = np.asarray(inputs["L0_ffn_w1"])
    w2 = np.asarray(inputs["L0_ffn_w2"])
    w3 = np.asarray(inputs["L0_ffn_w3"])
    sw1 = np.asarray(inputs["L0_ffn_sw1"])
    sw2 = np.asarray(inputs["L0_ffn_sw2"])
    sw3 = np.asarray(inputs["L0_ffn_sw3"])
    gate = np.asarray(inputs["L0_ffn_gate_weight"])

    w1t = np.concatenate([np.ascontiguousarray(w1.transpose(0, 2, 1)),
                          sw1.T.reshape(H, 2, IEXP).transpose(1, 0, 2)], 0)
    w3t = np.concatenate([np.ascontiguousarray(w3.transpose(0, 2, 1)),
                          sw3.T.reshape(H, 2, IEXP).transpose(1, 0, 2)], 0)
    w2t = np.concatenate([np.ascontiguousarray(w2.transpose(0, 2, 1)),
                          np.ascontiguousarray(sw2.T).reshape(2, IEXP, H)], 0)
    w1t = np.ascontiguousarray(w1t)
    w3t = np.ascontiguousarray(w3t)
    w2t = np.ascontiguousarray(w2t)
    gatet = np.ascontiguousarray(gate.T)

    pos = np.arange(SEQ, dtype=np.float64)
    invf = 1.0 / (1e6 ** (np.arange(0, DH, 2, dtype=np.float64) / DH))
    ang = pos[:, None] * invf[None, :]
    cosf = np.cos(ang).astype(np.float32)
    sinf = np.sin(ang).astype(np.float32)

    iden = np.eye(128).astype(BF)
    idf32 = np.eye(128).astype(np.float32)
    onesq = np.ones((128, 128), dtype=BF)
    onesrow = np.ones((1, SEQ), dtype=BF)
    blk8 = np.zeros((128, 128), np.float32)
    hid = (np.arange(128) % 32) // 2
    blk8[hid[:, None] == hid[None, :]] = 1.0

    if "nc" not in _CACHE:
        _CACHE["nc"] = build_nc()
    nc = _CACHE["nc"]

    common = dict(
        w1t=w1t, w3t=w3t, w2t=w2t, gatet=gatet,
        wq=np.asarray(inputs["L0_attn_Wq"]),
        wk=np.asarray(inputs["L0_attn_Wk"]),
        wv=np.asarray(inputs["L0_attn_Wv"]),
        wo=np.asarray(inputs["L0_attn_Wo"]),
        wr=np.asarray(inputs["L0_attn_Wr"]),
        bq=np.ascontiguousarray(np.asarray(inputs["L0_attn_bq"])[None, :]),
        bk=np.ascontiguousarray(np.asarray(inputs["L0_attn_bk"])[None, :]),
        bv=np.ascontiguousarray(np.asarray(inputs["L0_attn_bv"])[None, :]),
        bo=np.ascontiguousarray(np.asarray(inputs["L0_attn_bo"])[None, :]),
        br=np.ascontiguousarray(np.asarray(inputs["L0_attn_br"])[None, :]),
        iden=iden, idf32=idf32, onesq=onesq, onesrow=onesrow, blk8=blk8,
    )
    in_maps = []
    for c in range(8):
        b, r = c // 4, c % 4
        m = dict(common)
        m["x"] = np.ascontiguousarray(x[b, r * TLOC:(r + 1) * TLOC, :])
        sl = slice(r * TLOC, (r + 1) * TLOC)
        m["cosh"] = np.ascontiguousarray(cosf[sl])
        m["sinn"] = np.ascontiguousarray(-sinf[sl])
        m["sinp"] = np.ascontiguousarray(sinf[sl])
        in_maps.append(m)

    res = run_bass_kernel_spmd(nc, in_maps, core_ids=list(range(8)))
    out = np.empty((B, SEQ, H), np.float32)
    for c in range(8):
        b, r = c // 4, c % 4
        out[b, r * TLOC:(r + 1) * TLOC, :] = res.results[c]["out"]
    return out
